# version 18
# baseline (speedup 1.0000x reference)
import numpy as np
from collections import defaultdict

import concourse.bass as bass
import concourse.mybir as mybir
from concourse.bass_utils import run_bass_kernel_spmd

F32 = mybir.dt.float32
BF16 = mybir.dt.bfloat16
U8 = mybir.dt.uint8
AF = mybir.ActivationFunctionType
ALU = mybir.AluOpType
AX = mybir.AxisListType

D = 1024
ND = 8
FF = 4096
HD = 64
NHEAD = 16
NPAIR = 8
ER = 128
FC = 256
KC = FC // 128
NFC = FF // FC
TC = 512


class Ctr:
    __slots__ = ("sem", "v", "name")

    def __init__(self, nc, name):
        self.sem = nc.alloc_semaphore(name)
        self.v = 0
        self.name = name


class Prog:
    ENGINES = ("sync", "gpsimd", "tensor", "vector", "scalar")

    def __init__(self, nc, dedup=True):
        self.nc = nc
        self.dedup = dedup
        self.streams = defaultdict(list)
        self._waited = defaultdict(dict)
        self.fctr = {}
        self.dma_ctrs_sync = []
        self.dma_ctrs_gp = []
        self.own_ctr = {}

    def ctr(self, name):
        return Ctr(self.nc, name)

    def op(self, eng, fn, waits=(), inc=None, incn=1, noinc=False):
        own = self.own_ctr.get(eng)
        wlist = []
        if own is not None and own.v > 0:
            wlist.append((own, own.v))
        for w in waits:
            c, v = w if isinstance(w, tuple) else (w, w.v)
            if v > 0:
                wlist.append((c, v))
        if own is not None and not noinc:
            assert inc is None or inc is own
            inc = own
        if inc is not None:
            inc.v += incn

        def run(engine, _w=wlist, _inc=inc, _incn=incn, _fn=fn, _eng=eng):
            seen = self._waited[_eng]
            for c, v in _w:
                if self.dedup and seen.get(c.name, 0) >= v:
                    continue
                engine.wait_ge(c.sem, v)
                seen[c.name] = v
            inst = _fn(engine)
            if _inc is not None:
                inst.then_inc(_inc.sem, _incn)
            return inst

        self.streams[eng].append(run)

    def fence(self, engines=None):
        engines = engines or self.ENGINES
        for e in engines:
            if e not in self.fctr:
                self.fctr[e] = self.ctr("f_" + e)
        for e, ctrs in (("sync", self.dma_ctrs_sync), ("gpsimd", self.dma_ctrs_gp)):
            if e in engines:
                self.op(e, lambda eng: None, waits=[(c, c.v) for c in ctrs], noinc=True)
        for e in engines:
            if e in self.own_ctr:
                self.op(e, lambda eng: eng.nop())
            else:
                self.op(e, lambda eng: eng.nop(), inc=self.fctr[e])
        snaps = []
        for e in engines:
            c = self.own_ctr.get(e) or self.fctr[e]
            snaps.append((e, c, c.v))
        for e in engines:
            ws = [(c, v) for (e2, c, v) in snaps if e2 != e]
            self.op(e, lambda eng: None, waits=ws, noinc=True)

    def finalize(self):
        with self.nc.Block() as block:
            for eng in self.ENGINES:
                fns = self.streams.get(eng)
                if not fns:
                    continue

                def make(fns=fns):
                    def body(engine):
                        for f in fns:
                            f(engine)
                    return body

                getattr(block, eng)(make())


class Arena:
    def __init__(self, ap, nbytes):
        self.base = ap
        self.n = nbytes
        self.off = 0
        self.stack = []
        self.peak = 0

    def push(self):
        self.stack.append(self.off)

    def pop(self):
        self.off = self.stack.pop()

    def _al(self, parts, cols, esz, dt):
        self.off = (self.off + 31) & ~31
        nb = cols * esz
        v = self.base[:, self.off:self.off + nb].bitcast(dt)
        self.off += nb
        self.peak = max(self.peak, self.off)
        assert self.off <= self.n, f"arena overflow {self.off} > {self.n}"
        if parts != 128:
            v = v[0:parts, :]
        return v

    def f32(self, parts, cols):
        return self._al(parts, cols, 4, F32)

    def bf16(self, parts, cols):
        return self._al(parts, cols, 2, BF16)


def build_program(S=2048, NB=2, dedup=True):
    n_tt = S // 128
    HL = min(S, 1024)
    NQH = S // HL
    TH = min(S, 1024)
    NTH = S // TH
    n_tc = TH // TC
    n_qc = S // 512

    nc = bass.Bass("TRN2", target_bir_lowering=False)

    x_h = nc.dram_tensor("x", [NB, S, D], F32, kind="ExternalInput").ap()
    out_h = nc.dram_tensor("out", [NB, S, D], F32, kind="ExternalOutput").ap()

    def win(name, shape):
        return nc.dram_tensor(name, shape, F32, kind="ExternalInput").ap()

    wq_h = win("wq", [128, 8 * D])
    wk_h = win("wk", [128, 8 * D])
    wv_h = win("wv", [128, 8 * D])
    wo_h = win("wo", [128, 8 * D])
    bias_h = win("bias", [1, 4 * D])
    lncol_h = win("lncol", [128, 24])
    w1t_h = win("w1t", [128, NFC * 8 * FC])
    w3t_h = win("w3t", [128, NFC * 8 * FC])
    w2t_h = win("w2t", [128, NFC * KC * D])
    b1c_h = win("b1c", [128, FF])
    b3c_h = win("b3c", [128, FF])
    a2t_h = win("a2t", [128, NFC * KC * 128])
    FSC = 8 * FC + 8 * FC + KC * D + FC + FC + KC * 128
    wfs_h = win("wfs", [128, NFC * FSC])
    a1t_h = win("a1t", [128, ND * 128])
    a3t_h = win("a3t", [128, ND * 128])
    b2c_h = win("b2c", [128, ND * 128])
    gw_h = win("gw", [128, ND * 8])
    e2_h = win("e2", [2, 128])
    e8_h = win("e8", [8, 128])
    id_h = win("ident", [128, 128])

    xnsp = nc.dram_tensor("xnsp", [ND, 128, S], F32, kind="Internal").ap()
    nxsp = nc.dram_tensor("nxsp", [ND, 128, S], BF16, kind="Internal").ap()
    ssp = nc.dram_tensor("ssp", [ND, 128, S], F32, kind="Internal").ap()

    ones_row = nc.alloc_sbuf_tensor("ones_row", [1, 512], BF16).ap()
    ones1f = nc.alloc_sbuf_tensor("ones1f", [1, 128], F32).ap()
    onescf = nc.alloc_sbuf_tensor("onescf", [128, 64], F32).ap()
    ones64b = nc.alloc_sbuf_tensor("ones64b", [64, 1], BF16).ap()
    ind2 = nc.alloc_sbuf_tensor("ind2", [128, 2], BF16).ap()
    e2 = nc.alloc_sbuf_tensor("e2s", [2, 128], F32).ap()
    e8 = nc.alloc_sbuf_tensor("e8s", [8, 128], F32).ap()
    ident = nc.alloc_sbuf_tensor("idents", [128, 128], F32).ap()
    biasr = nc.alloc_sbuf_tensor("biasrs", [1, 4 * D], BF16).ap()
    lncol = nc.alloc_sbuf_tensor("lncols", [128, 24], F32).ap()
    gw = nc.alloc_sbuf_tensor("gws", [128, ND * 8], BF16).ap()
    a1t = nc.alloc_sbuf_tensor("a1ts", [128, ND * 128], BF16).ap()
    a3t = nc.alloc_sbuf_tensor("a3ts", [128, ND * 128], BF16).ap()
    b2c = nc.alloc_sbuf_tensor("b2cs", [128, ND * 128], BF16).ap()
    statc = nc.alloc_sbuf_tensor("statc", [128, 16 * 8], F32).ap()

    psa = nc.alloc_psum_tensor("psa", [128, 2048], F32).ap()
    psb = nc.alloc_psum_tensor("psb", [128, 2048], F32).ap()

    def bankA(i, cols=512):
        return psa[:, 512 * i:512 * i + cols]

    def bankB(i, cols=512):
        return psb[:, 512 * i:512 * i + cols]

    rem = nc._state.sbuf_bytes_remaining - 1024
    arena_t = nc.alloc_sbuf_tensor("arena", [128, rem], U8).ap()
    ar = Arena(arena_t, rem)

    p = Prog(nc, dedup=dedup)
    tc_ = p.ctr("tc")
    vc = p.ctr("vc")
    sc = p.ctr("sc")
    c_ld = p.ctr("c_ld")
    c_lg = p.ctr("c_lg")
    xld = [p.ctr("xld0"), p.ctr("xld1")]
    stsp = [p.ctr(f"stsp{c}") for c in range(ND)]
    wql = [p.ctr("wql0"), p.ctr("wql1")]
    wkl = [p.ctr("wkl0"), p.ctr("wkl1")]
    wvl = [p.ctr("wvl0"), p.ctr("wvl1")]
    wol = [p.ctr("wol0"), p.ctr("wol1")]
    xnl = [p.ctr("xnl0"), p.ctr("xnl1")]
    ssl_ = [p.ctr("ssl0"), p.ctr("ssl1")]
    nxl = p.ctr("nxl")
    w1l = [p.ctr("w1l0"), p.ctr("w1l1")]
    w3l = [p.ctr("w3l0"), p.ctr("w3l1")]
    w2l = [p.ctr("w2l0"), p.ctr("w2l1")]
    bl1 = [p.ctr("bl10"), p.ctr("bl11")]
    bl3 = [p.ctr("bl30"), p.ctr("bl31")]
    a2l = [p.ctr("a2l0"), p.ctr("a2l1")]
    stl = [p.ctr("stl0"), p.ctr("stl1")]
    xfl = [p.ctr("xfl0"), p.ctr("xfl1")]
    ofs = [p.ctr("ofs0"), p.ctr("ofs1")]
    p.own_ctr = {"vector": vc, "scalar": sc}
    p.dma_ctrs_sync = [c_ld] + xld + stsp + xnl + ssl_ + [nxl] + stl + xfl + ofs
    p.dma_ctrs_gp = [c_lg] + wql + wkl + wvl + wol + w1l + w3l + w2l + bl1 + bl3 + a2l

    def T(fn, waits=(), inc=None, incn=1):
        p.op("tensor", fn, waits, inc, incn)

    def V(fn, waits=(), inc=None, incn=1):
        p.op("vector", fn, waits, inc, incn)

    def SC(fn, waits=(), inc=None, incn=1):
        p.op("scalar", fn, waits, inc, incn)

    def Y(fn, waits=(), inc=None, incn=1):
        p.op("sync", fn, waits, inc, incn)

    def G(fn, waits=(), inc=None, incn=1):
        p.op("gpsimd", fn, waits, inc, incn)

    V(lambda e: e.memset(ones_row[:, :], 1.0), inc=vc)
    V(lambda e: e.memset(ones64b[:, :], 1.0), inc=vc)
    V(lambda e: e.memset(ones1f[:, :], 1.0), inc=vc)
    V(lambda e: e.memset(onescf[:, :], 1.0), inc=vc)
    V(lambda e: e.memset(ind2[:, :], 0.0), inc=vc)
    V(lambda e: e.memset(ind2[0:64, 0:1], 1.0), waits=[vc], inc=vc)
    V(lambda e: e.memset(ind2[64:128, 1:2], 1.0), waits=[vc], inc=vc)
    Y(lambda e: e.dma_start(out=e2, in_=e2_h), inc=c_ld, incn=16)
    Y(lambda e: e.dma_start(out=e8, in_=e8_h), inc=c_ld, incn=16)
    Y(lambda e: e.dma_start(out=ident, in_=id_h), inc=c_ld, incn=16)
    Y(lambda e: e.dma_start(out=lncol, in_=lncol_h), inc=c_ld, incn=16)
    G(lambda e: e.dma_start(out=biasr, in_=bias_h), inc=c_lg, incn=16)
    G(lambda e: e.dma_start(out=gw, in_=gw_h), inc=c_lg, incn=16)
    G(lambda e: e.dma_start(out=a1t, in_=a1t_h), inc=c_lg, incn=16)
    G(lambda e: e.dma_start(out=a3t, in_=a3t_h), inc=c_lg, incn=16)
    G(lambda e: e.dma_start(out=b2c, in_=b2c_h), inc=c_lg, incn=16)
    consts_v = vc.v
    consts_ld = (c_ld, c_ld.v)
    consts_lg = (c_lg, c_lg.v)

    for b in range(NB):
        ar.push()
        x2T = ar.bf16(128, ND * S)
        qbuf = ar.bf16(128, ND * S)
        kbuf = ar.bf16(128, ND * S)
        vbuf = ar.bf16(128, n_tt * NHEAD * 65 + 63)

        ar.push()
        xt = [ar.f32(128, D) for _ in range(2)]
        sqf = ar.f32(128, D)
        xnb = ar.f32(128, D)
        x2p = ar.f32(128, D)
        nxp = ar.f32(128, D)
        xnstg = [ar.f32(128, 512) for _ in range(ND)]
        nxstg = [ar.bf16(128, 512) for _ in range(ND)]

        xrel = [(vc, 0), (vc, 0)]
        for i in range(n_tt):
            sl = i % 2
            Y(lambda e, sl=sl, i=i, b=b: e.dma_start(out=xt[sl], in_=x_h[b, 128 * i:128 * (i + 1), :]),
              waits=[xrel[sl]], inc=xld[sl], incn=16)
            ldw = (xld[sl], xld[sl].v)
            SC(lambda e, sl=sl: e.activation(sqf, xt[sl], AF.Square),
               waits=[ldw, (vc, vc.v)], inc=sc)
            V(lambda e, sl=sl, i=i: e.reduce_sum(statc[:, i:i + 1], xt[sl], axis=AX.X),
              waits=[ldw], inc=vc)
            V(lambda e, i=i: e.reduce_sum(statc[:, 16 + i:17 + i], sqf, axis=AX.X),
              waits=[(sc, sc.v)], inc=vc)
            xrel[sl] = (vc, vc.v)

        cs = statc

        def col(k):
            return cs[:, 16 * k:16 * k + n_tt]

        V(lambda e: e.tensor_scalar(col(2), col(0), 1.0 / D, None, ALU.mult), waits=[vc], inc=vc)
        V(lambda e: e.tensor_tensor(col(7), col(2), col(2), ALU.mult), waits=[vc], inc=vc)
        V(lambda e: e.tensor_scalar(col(3), col(1), 1.0 / D, None, ALU.mult), waits=[vc], inc=vc)
        V(lambda e: e.tensor_tensor(col(3), col(3), col(7), ALU.subtract), waits=[vc], inc=vc)
        V(lambda e: e.tensor_scalar(col(7), col(3), 1e-5, None, ALU.add), waits=[vc], inc=vc)
        SC(lambda e: e.activation(col(4), col(7), AF.Ln), waits=[vc], inc=sc)
        SC(lambda e: e.activation(col(4), col(4), AF.Exp, scale=-0.5), waits=[sc], inc=sc)
        V(lambda e: e.reciprocal(col(5), col(7)), waits=[vc, sc], inc=vc)
        V(lambda e: e.tensor_tensor(col(5), col(3), col(5), ALU.mult), waits=[vc], inc=vc)
        V(lambda e: e.tensor_scalar(col(6), col(5), 1e-6, None, ALU.add), waits=[vc], inc=vc)
        SC(lambda e: e.activation(col(6), col(6), AF.Ln), waits=[vc], inc=sc)
        SC(lambda e: e.activation(col(6), col(6), AF.Exp, scale=-0.5), waits=[sc], inc=sc)
        V(lambda e: e.tensor_scalar(col(5), col(5), 1e-5, None, ALU.add), waits=[sc], inc=vc)
        SC(lambda e: e.activation(col(5), col(5), AF.Ln), waits=[vc], inc=sc)
        SC(lambda e: e.activation(col(5), col(5), AF.Exp, scale=-0.5), waits=[sc], inc=sc)
        V(lambda e: e.tensor_tensor(col(5), col(4), col(5), ALU.mult), waits=[sc], inc=vc)
        V(lambda e: e.tensor_tensor(col(6), col(4), col(6), ALU.mult), waits=[vc], inc=vc)
        stats_done = (vc, vc.v)

        stg_rel = [(stsp[c], 0) for c in range(ND)]
        xrelA = list(xrel)
        trA_rel = [(sc, 0), (sc, 0), (vc, 0), (vc, 0)]
        trB_rel = [(vc, 0), (vc, 0)]
        for i in range(n_tt):
            sl = i % 2
            Y(lambda e, sl=sl, i=i, b=b: e.dma_start(out=xt[sl], in_=x_h[b, 128 * i:128 * (i + 1), :]),
              waits=[xrelA[sl]], inc=xld[sl], incn=16)
            ldw = (xld[sl], xld[sl].v)
            V(lambda e, sl=sl, i=i: e.tensor_scalar(
                xnb, xt[sl], cs[:, 32 + i:33 + i], cs[:, 64 + i:65 + i], ALU.subtract, ALU.mult),
              waits=[ldw, stats_done, (tc_, tc_.v)], inc=vc)
            V(lambda e, sl=sl, i=i: e.tensor_scalar(
                x2p, xt[sl], cs[:, 32 + i:33 + i], cs[:, 80 + i:81 + i], ALU.subtract, ALU.mult),
              waits=[(tc_, tc_.v)], inc=vc)
            V(lambda e, sl=sl, i=i: e.tensor_scalar(
                nxp, xt[sl], cs[:, 32 + i:33 + i], cs[:, 96 + i:97 + i], ALU.subtract, ALU.mult),
              waits=[(tc_, tc_.v)], inc=vc)
            xrelA[sl] = (vc, vc.v)
            apv = vc.v
            for c in range(ND):
                tb = c % 2
                T(lambda e, c=c, tb=tb: e.transpose(
                    bankA(tb, 128), xnb[:, 128 * c:128 * (c + 1)], ident),
                  waits=[(vc, apv), trA_rel[tb], consts_ld], inc=tc_)
                t1 = tc_.v
                T(lambda e, c=c, tb=tb: e.transpose(
                    bankA(2 + tb, 128), x2p[:, 128 * c:128 * (c + 1)], ident),
                  waits=[trA_rel[2 + tb]], inc=tc_)
                t2 = tc_.v
                T(lambda e, c=c, tb=tb: e.transpose(
                    bankB(tb, 128), nxp[:, 128 * c:128 * (c + 1)], ident),
                  waits=[trB_rel[tb]], inc=tc_)
                t3 = tc_.v
                SC(lambda e, c=c, i=i, tb=tb: e.activation(
                    xnstg[c][:, 128 * (i % 4):128 * (i % 4 + 1)], bankA(tb, 128), AF.Copy),
                   waits=[(tc_, t1), stg_rel[c]], inc=sc)
                trA_rel[tb] = (sc, sc.v)
                V(lambda e, c=c, i=i, tb=tb: e.tensor_scalar(
                    x2T[:, S * c + 128 * i: S * c + 128 * (i + 1)], bankA(2 + tb, 128),
                    lncol[:, c:c + 1], lncol[:, 8 + c:9 + c], ALU.mult, ALU.add),
                  waits=[(tc_, t2)], inc=vc)
                trA_rel[2 + tb] = (vc, vc.v)
                V(lambda e, c=c, i=i, tb=tb: e.tensor_scalar(
                    nxstg[c][:, 128 * (i % 4):128 * (i % 4 + 1)], bankB(tb, 128),
                    lncol[:, 16 + c:17 + c], None, ALU.mult),
                  waits=[(tc_, t3), stg_rel[c]], inc=vc)
                trB_rel[tb] = (vc, vc.v)
            if i % 4 == 3:
                for c in range(ND):
                    Y(lambda e, c=c, i=i: e.dma_start(
                        out=xnsp[c, :, 512 * (i // 4):512 * (i // 4) + 512], in_=xnstg[c]),
                      waits=[(sc, sc.v), (vc, vc.v)], inc=stsp[c], incn=16)
                    Y(lambda e, c=c, i=i: e.dma_start(
                        out=nxsp[c, :, 512 * (i // 4):512 * (i // 4) + 512], in_=nxstg[c]),
                      waits=[(vc, vc.v)], inc=stsp[c], incn=16)
                    stg_rel[c] = (stsp[c], stsp[c].v)
        a_spill_done = [(stsp[c], stsp[c].v) for c in range(ND)]
        x2t_done = (vc, vc.v)
        ar.pop()
        p.fence()

        ar.push()
        wsl = [[ar.bf16(128, D) for _ in range(2)] for _ in range(3)]
        qraw = [ar.f32(128, 512) for _ in range(2)]
        kraw = [ar.f32(128, 512) for _ in range(2)]
        rowa = ar.f32(2, 512)
        rowb_ = ar.f32(2, 512)
        sqs = ar.bf16(128, 512)
        wrel = [[(tc_, 0)] * 2 for _ in range(3)]
        bstate = {"pq": [[], []], "pk": [[], []], "pv": [[], []], "gqc": [0]}
        btail_pending = [None]
        for i in range(n_tt):
            V(lambda e, i=i: e.memset(vbuf[:, 1040 * i:1040 * (i + 1)], 1.0),
              waits=[(tc_, tc_.v)], inc=vc)
        V(lambda e: e.memset(vbuf[:, 1040 * n_tt:1040 * n_tt + 63], 1.0),
          waits=[(tc_, tc_.v)], inc=vc)
        vbuf_init = (vc, vc.v)

        for h in range(NPAIR):
            sl = h % 2
            wsems = (wql, wkl, wvl)
            for wi, wh in enumerate((wq_h, wk_h, wv_h)):
                G(lambda e, sl=sl, wi=wi, wh=wh, h=h: e.dma_start(
                    out=wsl[wi][sl], in_=wh[:, D * h:D * (h + 1)]),
                  waits=[wrel[wi][sl]], inc=wsems[wi][sl], incn=16)
            wq_ld = (wql[sl], wql[sl].v)
            wk_ld = (wkl[sl], wkl[sl].v)
            wv_ld = (wvl[sl], wvl[sl].v)
            bq_lh = biasr[0:1, 128 * h:128 * (h + 1)]
            bk_lh = biasr[0:1, D + 128 * h:D + 128 * (h + 1)]
            bv_rh = biasr[0:1, 2 * D + 128 * h:2 * D + 128 * (h + 1)]
            pq_rel = bstate["pq"]
            pk_rel = bstate["pk"]
            pv_rel = bstate["pv"]

            def flush_btail(info):
                h3, bqv_, qc2, qsl2, vsl2, q_ready2, k_ready2, tv2, cols2 = info
                for tt4 in range(4):
                    i2 = 4 * qc2 + tt4
                    for hh2 in range(2):
                        V(lambda e, i2=i2, tt4=tt4, vsl2=vsl2, hh2=hh2, h=h3: e.tensor_copy(
                            vbuf[:, 1040 * i2 + 65 * (2 * h + hh2):1040 * i2 + 65 * (2 * h + hh2) + 64],
                            psb[:, 512 * vsl2 + 128 * tt4 + 64 * hh2:512 * vsl2 + 128 * tt4 + 64 * hh2 + 64]),
                          waits=[(tc_, tv2), vbuf_init])
                pv_rel[vsl2] = [(vc, vc.v)]
                V(lambda e, qsl2=qsl2: e.tensor_copy(qraw[qsl2], bankA(qsl2)),
                  waits=[(tc_, q_ready2)])
                qcopy_v = vc.v
                SC(lambda e, qsl2=qsl2: e.activation(sqs, bankA(qsl2), AF.Square),
                   waits=[(tc_, q_ready2), (tc_, tc_.v), (vc, vc.v)], inc=sc)
                T(lambda e: e.matmul(psb[0:2, 1024:1536], ind2, sqs, start=True, stop=True),
                  waits=[(sc, sc.v), (vc, vc.v)], inc=tc_)
                V(lambda e: e.tensor_scalar(rowa, psb[0:2, 1024:1536], 1e-24, None, ALU.max),
                  waits=[(tc_, tc_.v), (sc, sc.v)])
                SC(lambda e: e.activation(rowb_, rowa, AF.Ln), waits=[(vc, vc.v)], inc=sc)
                SC(lambda e: e.activation(rowa, rowb_, AF.Exp, scale=-0.5), inc=sc)
                T(lambda e: e.matmul(bankB(2), e2, rowa[0:2, :], start=True, stop=True),
                  waits=[(sc, sc.v), (vc, vc.v), consts_ld], inc=tc_)
                V(lambda e, h=h3, cols2=cols2, qsl2=qsl2: e.tensor_tensor(
                    qbuf[:, S * h + cols2.start:S * h + cols2.stop], qraw[qsl2], bankB(2), ALU.mult),
                  waits=[(tc_, tc_.v)])
                pq_rel[qsl2] = [(sc, sc.v), (vc, qcopy_v)]
                V(lambda e, qsl2=qsl2: e.tensor_copy(kraw[qsl2], bankA(2 + qsl2)),
                  waits=[(tc_, k_ready2)])
                kcopy_v = vc.v
                SC(lambda e, qsl2=qsl2: e.activation(sqs, bankA(2 + qsl2), AF.Square),
                   waits=[(tc_, k_ready2), (vc, vc.v), (tc_, tc_.v)], inc=sc)
                T(lambda e: e.matmul(psb[0:2, 1536:2048], ind2, sqs, start=True, stop=True),
                  waits=[(sc, sc.v)], inc=tc_)
                V(lambda e: e.tensor_scalar(rowa, psb[0:2, 1536:2048], 1e-24, None, ALU.max),
                  waits=[(tc_, tc_.v), (sc, sc.v)])
                SC(lambda e: e.activation(rowb_, rowa, AF.Ln), waits=[(vc, vc.v)], inc=sc)
                SC(lambda e: e.activation(rowa, rowb_, AF.Exp, scale=-0.5), inc=sc)
                T(lambda e: e.matmul(bankB(3), e2, rowa[0:2, :], start=True, stop=True),
                  waits=[(sc, sc.v), (vc, vc.v)], inc=tc_)
                V(lambda e, h=h3, cols2=cols2, qsl2=qsl2: e.tensor_tensor(
                    kbuf[:, S * h + cols2.start:S * h + cols2.stop], kraw[qsl2], bankB(3), ALU.mult),
                  waits=[(tc_, tc_.v)])
                pk_rel[qsl2] = [(sc, sc.v), (vc, kcopy_v)]

            for qc in range(n_qc):
                qsl = bstate["gqc"][0] % 2
                bstate["gqc"][0] += 1
                cols = slice(512 * qc, 512 * (qc + 1))
                for c in range(ND):
                    T(lambda e, c=c, qsl=qsl, cols=cols, sl=sl: e.matmul(
                        bankA(qsl), wsl[0][sl][:, 128 * c:128 * (c + 1)],
                        x2T[:, S * c + cols.start:S * c + cols.stop],
                        start=(c == 0), stop=False),
                      waits=([wq_ld, x2t_done, (vc, consts_v), consts_lg] + pq_rel[qsl]) if c == 0 else [])
                T(lambda e, qsl=qsl, bq_lh=bq_lh: e.matmul(
                    bankA(qsl), bq_lh, ones_row, start=False, stop=True), inc=tc_)
                q_ready = tc_.v
                for c in range(ND):
                    T(lambda e, c=c, qsl=qsl, cols=cols, sl=sl: e.matmul(
                        bankA(2 + qsl), wsl[1][sl][:, 128 * c:128 * (c + 1)],
                        x2T[:, S * c + cols.start:S * c + cols.stop],
                        start=(c == 0), stop=False),
                      waits=([wk_ld] + pk_rel[qsl]) if c == 0 else [])
                T(lambda e, qsl=qsl, bk_lh=bk_lh: e.matmul(
                    bankA(2 + qsl), bk_lh, ones_row, start=False, stop=True), inc=tc_)
                k_ready = tc_.v
                vsl = qsl
                for tt4 in range(4):
                    i = 4 * qc + tt4
                    for c in range(ND):
                        T(lambda e, c=c, i=i, tt4=tt4, vsl=vsl, sl=sl: e.matmul(
                            psb[:, 512 * vsl + 128 * tt4:512 * vsl + 128 * (tt4 + 1)],
                            x2T[:, S * c + 128 * i:S * c + 128 * (i + 1)],
                            wsl[2][sl][:, 128 * c:128 * (c + 1)],
                            start=(c == 0), stop=False),
                          waits=([wv_ld] + pv_rel[vsl]) if (c == 0 and tt4 == 0) else [])
                    T(lambda e, tt4=tt4, vsl=vsl, bv_rh=bv_rh: e.matmul(
                        psb[:, 512 * vsl + 128 * tt4:512 * vsl + 128 * (tt4 + 1)],
                        ones_row[0:1, 0:128], bv_rh, start=False, stop=True),
                      inc=(tc_ if tt4 == 3 else None))
                tv = tc_.v
                if btail_pending[0] is not None:
                    fn0, info0 = btail_pending[0]
                    fn0(info0)
                btail_pending[0] = (flush_btail, (h, 0, qc, qsl, vsl, q_ready, k_ready, tv, cols))
            for wi in range(3):
                wrel[wi][sl] = (tc_, tc_.v)
                wrel[wi][sl] = (tc_, tc_.v)
        if btail_pending[0] is not None:
            fn0, info0 = btail_pending[0]
            fn0(info0)
            btail_pending[0] = None
        ar.pop()
        p.fence()

        ar.push()
        kaugs = [ar.bf16(128, S) for _ in range(2)]
        staugs = [ar.bf16(128, S) for _ in range(2)]
        att = [ar.bf16(128, HL) for _ in range(4)]
        sqc = ar.bf16(64, S)
        srow2 = ar.bf16(1, S)
        bcf = ar.f32(64, HL)
        stg = [ar.f32(65, S) for _ in range(2)]
        for z in range(2):
            V(lambda e, z=z: e.memset(kaugs[z][64:128, :], 0.0), waits=[(tc_, tc_.v)])
            V(lambda e, z=z: e.memset(kaugs[z][64:65, :], 1.0))
            V(lambda e, z=z: e.memset(staugs[z][64:128, :], 0.0))
            V(lambda e, z=z: e.memset(staugs[z][64:65, :], -1.0))

        def head_copies(hh, step):
            z = hh % 2
            ht, sub = hh // 2, hh % 2
            V(lambda e, ht=ht, sub=sub, z=z: e.tensor_copy(
                kaugs[z][0:64, :], kbuf[64 * sub:64 * sub + 64, S * ht:S * (ht + 1)]),
              waits=[(tc_, tc_.v)])
            V(lambda e, ht=ht, sub=sub, z=z: e.tensor_copy(
                staugs[z][0:64, :], qbuf[64 * sub:64 * sub + 64, S * ht:S * (ht + 1)]))
            if step == 1:
                SC(lambda e, ht=ht, sub=sub: e.activation(
                    sqc, qbuf[64 * sub:64 * sub + 64, S * ht:S * (ht + 1)], AF.Square),
                   waits=[(vc, vc.v), (tc_, tc_.v)], inc=sc)
                for j4 in range(S // 512):
                    T(lambda e, j4=j4: e.matmul(
                        psa[0:1, 1536:2048], ones64b,
                        sqc[:, 512 * j4:512 * (j4 + 1)], start=True, stop=True),
                      waits=[(sc, sc.v)], inc=tc_)
                    SC(lambda e, j4=j4: e.activation(
                        srow2[0:1, 512 * j4:512 * (j4 + 1)],
                        psa[0:1, 1536:2048], AF.Ln),
                       waits=[(tc_, tc_.v)], inc=sc)
                SC(lambda e: e.activation(srow2, srow2, AF.Exp, scale=0.5),
                   waits=[(sc, sc.v)], inc=sc)
                V(lambda e, z=z: e.tensor_scalar(
                    staugs[z][64:65, :], srow2, -1.0, None, ALU.mult),
                  waits=[(sc, sc.v)])
            return (vc, vc.v)

        tail_rel = [(vc, 0)]
        sim_rel = [[(sc, sc.v)], [(sc, sc.v)]]
        att_rel = [[(tc_, tc_.v)] for _ in range(4)]
        acc_gate = [[(vc, vc.v)]]
        for step in range(2):
            heads_ready = [None] * (NHEAD + 1)
            heads_ready[0] = head_copies(0, step)
            pending_tail = None

            def flush_tail(info):
                hh2, z2, stg_v = info
                ht2, sub2 = hh2 // 2, hh2 % 2
                for qh2 in range(NQH):
                    qco2 = HL * qh2
                    for j2 in range(HL // 512):
                        T(lambda e, z2=z2, qco2=qco2, j2=j2: e.matmul(
                            psa[0:64, 1024:1536], onescf[64:65, :],
                            stg[z2][64:65, qco2 + 512 * j2:qco2 + 512 * (j2 + 1)],
                            start=True, stop=True),
                          waits=[(sc, stg_v[1]), (vc, vc.v)], inc=tc_)
                        V(lambda e, j2=j2: e.tensor_copy(bcf[:, 512 * j2:512 * (j2 + 1)],
                                                         psa[0:64, 1024:1536]),
                          waits=[(tc_, tc_.v)])
                    V(lambda e, z2=z2, qco2=qco2, ht2=ht2, sub2=sub2: e.tensor_tensor(
                        qbuf[64 * sub2:64 * sub2 + 64, S * ht2 + qco2:S * ht2 + qco2 + HL],
                        bcf, stg[z2][0:64, qco2:qco2 + HL], ALU.mult),
                      waits=[(vc, vc.v)])

            for hh in range(NHEAD):
                z = hh % 2
                kaug = kaugs[z]
                stin = staugs[z]
                head_in = heads_ready[hh]
                if hh + 1 < NHEAD:
                    heads_ready[hh + 1] = head_copies(hh + 1, step)
                for qh in range(NQH):
                    qco = HL * qh

                    def emit_sim(c):
                        sb = c % 2
                        for j2 in range(HL // 512):
                            T(lambda e, c=c, sb=sb, j2=j2, qco=qco, stin=stin, kaug=kaug: e.matmul(
                                psb[:, 1024 * sb + 512 * j2:1024 * sb + 512 * (j2 + 1)],
                                kaug[0:128, 128 * c:128 * (c + 1)],
                                stin[0:128, qco + 512 * j2:qco + 512 * (j2 + 1)],
                                start=True, stop=True),
                              waits=([head_in] + sim_rel[sb]) if j2 == 0 else [],
                              inc=(tc_ if j2 == HL // 512 - 1 else None))
                        return tc_.v

                    simv = [0] * n_tt
                    simv[0] = emit_sim(0)
                    if n_tt > 1:
                        simv[1] = emit_sim(1)
                    for c in range(n_tt):
                        sb = c % 2
                        ab = c % 4
                        SC(lambda e, ab=ab, sb=sb: e.activation(att[ab], psb[:, 1024 * sb:1024 * sb + HL],
                                                                AF.Exp),
                           waits=[(tc_, simv[c])] + att_rel[ab], inc=sc)
                        scv = sc.v
                        sim_rel[sb] = [(sc, scv)]
                        for j2 in range(HL // 512):
                            T(lambda e, c=c, ab=ab, j2=j2, hh=hh: e.matmul(
                                psa[0:128, 512 * j2:512 * (j2 + 1)],
                                vbuf[:, 1040 * c + 65 * hh:1040 * c + 65 * hh + 128],
                                att[ab][:, 512 * j2:512 * (j2 + 1)],
                                start=(c == 0), stop=(c == n_tt - 1)),
                              waits=([(sc, scv)] + (acc_gate[0] if c == 0 else [])) if j2 == 0 else [],
                              inc=(tc_ if (c == n_tt - 1 and j2 == HL // 512 - 1) else None))
                        att_rel[ab] = [(tc_, tc_.v)]
                        if c + 2 < n_tt:
                            simv[c + 2] = emit_sim(c + 2)
                    st_ready = tc_.v
                    V(lambda e, z=z, qco=qco: e.tensor_copy(
                        stg[z][:, qco:qco + HL], psa[0:65, 0:HL]),
                      waits=[(tc_, st_ready)])
                    acc_gate[0] = [(vc, vc.v)]
                    SC(lambda e, z=z, qco=qco: e.activation(
                        stg[z][64:65, qco:qco + HL], stg[z][64:65, qco:qco + HL], AF.Ln),
                       waits=[(vc, vc.v), (tc_, tc_.v)], inc=sc)
                    SC(lambda e, z=z, qco=qco: e.activation(
                        stg[z][64:65, qco:qco + HL], stg[z][64:65, qco:qco + HL],
                        AF.Exp, scale=-1.0), inc=sc)
                stg_v = (vc.v, sc.v)
                if pending_tail is not None:
                    flush_tail(pending_tail)
                pending_tail = (hh, z, stg_v)
            flush_tail(pending_tail)
        ar.pop()
        p.fence()

        ar.push()
        wosl = [ar.bf16(128, D) for _ in range(2)]
        xnrt = [ar.f32(128, S) for _ in range(2)]
        sstg = [ar.f32(128, 512) for _ in range(2)]
        worel = [(tc_, 0), (tc_, 0)]
        xnrel = [(vc, 0), (vc, 0)]
        srel = [(ssl_[0], 0), (ssl_[1], 0)]
        for dd in range(ND):
            sl = dd % 2
            G(lambda e, sl=sl, dd=dd: e.dma_start(out=wosl[sl], in_=wo_h[:, D * dd:D * (dd + 1)]),
              waits=[worel[sl]], inc=wol[sl], incn=16)
            wo_ld = (wol[sl], wol[sl].v)
            Y(lambda e, sl=sl, dd=dd: e.dma_start(out=xnrt[sl], in_=xnsp[dd]),
              waits=[xnrel[sl]] + a_spill_done, inc=xnl[sl], incn=16)
            xn_ld = (xnl[sl], xnl[sl].v)
            bo_lh = biasr[0:1, 3 * D + 128 * dd:3 * D + 128 * (dd + 1)]
            for qc in range(n_qc):
                ob = qc % 2
                cols = slice(512 * qc, 512 * (qc + 1))
                for c in range(ND):
                    T(lambda e, c=c, ob=ob, cols=cols, sl=sl: e.matmul(
                        bankA(ob), wosl[sl][:, 128 * c:128 * (c + 1)],
                        qbuf[:, S * c + cols.start:S * c + cols.stop],
                        start=(c == 0), stop=False),
                      waits=[wo_ld, (vc, vc.v)] if c == 0 else [])
                T(lambda e, ob=ob, bo_lh=bo_lh: e.matmul(
                    bankA(ob), bo_lh, ones_row, start=False, stop=True), inc=tc_)
                tv = tc_.v
                ssl = qc % 2
                V(lambda e, ob=ob, ssl=ssl, cols=cols, sl=sl: e.tensor_tensor(
                    sstg[ssl], bankA(ob), xnrt[sl][:, cols], ALU.add),
                  waits=[(tc_, tv), xn_ld, srel[ssl]], inc=vc)
                Y(lambda e, ssl=ssl, dd=dd, cols=cols: e.dma_start(
                    out=ssp[dd, :, cols.start:cols.stop], in_=sstg[ssl]),
                  waits=[(vc, vc.v)], inc=ssl_[ssl], incn=16)
                srel[ssl] = (ssl_[ssl], ssl_[ssl].v)
            worel[sl] = (tc_, tc_.v)
            xnrel[sl] = (vc, vc.v)
        s_spill_done = [(ssl_[0], ssl_[0].v), (ssl_[1], ssl_[1].v)]
        ar.pop()
        ar.pop()

        p.fence()
        ar.push()
        outb = ar.f32(128, ND * TH)
        for th in range(NTH):
            tho = TH * th
            ar.push()
            nxh = ar.bf16(128, ND * TH)
            wfsl = [ar.bf16(128, FSC) for _ in range(2)]
            wsl1 = [t[:, 0:8 * FC] for t in wfsl]
            wsl3 = [t[:, 8 * FC:16 * FC] for t in wfsl]
            wsl2 = [t[:, 16 * FC:16 * FC + KC * D] for t in wfsl]
            b1sl = [t[:, 16 * FC + KC * D:17 * FC + KC * D] for t in wfsl]
            b3sl = [t[:, 17 * FC + KC * D:18 * FC + KC * D] for t in wfsl]
            a2sl = [t[:, 18 * FC + KC * D:18 * FC + KC * D + KC * 128] for t in wfsl]
            trsel = [[ar.bf16(128, TH) for _ in range(2)] for _ in range(2)]
            wexp = [ar.f32(128, TH) for _ in range(2)]
            mex = [ar.f32(128, TH) for _ in range(2)]
            rrs = [ar.f32(128, TH) for _ in range(2)]
            rrb = [ar.bf16(128, TH) for _ in range(2)]
            act2w = [ar.bf16(128, KC * TC) for _ in range(2)]
            silu_s = ar.f32(128, KC * TC)
            h3w_s = ar.f32(128, KC * TC)
            glog = ar.f32(128, 8)
            m1c = ar.f32(128, 1)
            m2c = ar.f32(128, 1)
            e2c = ar.f32(128, 1)
            w1c = ar.f32(128, 1)
            w2c_ = ar.f32(128, 1)
            msk = [ar.f32(128, 8) for _ in range(2)]
            tmp8 = ar.f32(128, 8)
            mrow = [ar.f32(8, TH) for _ in range(2)]
            wrow = [ar.f32(1, TH) for _ in range(2)]

            for c in range(ND):
                Y(lambda e, c=c, tho=tho: e.dma_start(
                    out=nxh[:, TH * c:TH * (c + 1)], in_=nxsp[c, :, tho:tho + TH]),
                  waits=a_spill_done, inc=nxl, incn=16)
            nx_ld = (nxl, nxl.v)

            for jt in range(TH // 128):
                gb = jt % 2
                for c in range(ND):
                    T(lambda e, c=c, jt=jt, gb=gb: e.matmul(
                        psa[:, 512 * gb:512 * gb + 8],
                        nxh[:, TH * c + 128 * jt:TH * c + 128 * (jt + 1)],
                        gw[:, 8 * c:8 * (c + 1)],
                        start=(c == 0), stop=(c == ND - 1)),
                      waits=[nx_ld, (vc, vc.v), consts_lg] if c == 0 else [],
                      inc=(tc_ if c == ND - 1 else None))
                gv = tc_.v
                V(lambda e, gb=gb: e.tensor_copy(glog, psa[:, 512 * gb:512 * gb + 8]),
                  waits=[(tc_, gv), (vc, vc.v)], inc=vc)
                V(lambda e: e.reduce_max(m1c, glog, axis=AX.X), waits=[(vc, vc.v)], inc=vc)
                V(lambda e: e.tensor_scalar(msk[0], glog, m1c, None, ALU.is_equal),
                  waits=[(vc, vc.v)], inc=vc)
                V(lambda e: e.tensor_scalar(tmp8, msk[0], -1e30, None, ALU.mult),
                  waits=[(vc, vc.v)], inc=vc)
                V(lambda e: e.tensor_tensor(tmp8, glog, tmp8, ALU.add), waits=[(vc, vc.v)], inc=vc)
                V(lambda e: e.reduce_max(m2c, tmp8, axis=AX.X), waits=[(vc, vc.v)], inc=vc)
                V(lambda e: e.tensor_scalar(msk[1], glog, m2c, None, ALU.is_equal),
                  waits=[(vc, vc.v)], inc=vc)
                V(lambda e: e.tensor_tensor(e2c, m2c, m1c, ALU.subtract), waits=[(vc, vc.v)], inc=vc)
                SC(lambda e: e.activation(e2c, e2c, AF.Exp), waits=[(vc, vc.v)], inc=sc)
                V(lambda e: e.tensor_scalar(w1c, e2c, 1.0, None, ALU.add), waits=[(sc, sc.v)], inc=vc)
                V(lambda e: e.reciprocal(w1c, w1c), waits=[(vc, vc.v)], inc=vc)
                V(lambda e: e.tensor_tensor(w2c_, e2c, w1c, ALU.mult), waits=[(vc, vc.v)], inc=vc)
                mv = vc.v
                for j in range(2):
                    T(lambda e, j=j: e.transpose(
                        psa[0:8, 1024 + 128 * j:1024 + 128 * (j + 1)], msk[j], ident),
                      waits=[(vc, mv), (vc, vc.v)], inc=tc_)
                    V(lambda e, j=j, jt=jt: e.tensor_copy(
                        mrow[j][:, 128 * jt:128 * (jt + 1)],
                        psa[0:8, 1024 + 128 * j:1024 + 128 * (j + 1)]),
                      waits=[(tc_, tc_.v)], inc=vc)
                for j, wcol in enumerate((w1c, w2c_)):
                    T(lambda e, j=j, wcol=wcol: e.transpose(
                        psa[0:1, 1536 + 128 * j:1536 + 128 * (j + 1)], wcol, ident),
                      waits=[(vc, mv), (vc, vc.v)], inc=tc_)
                    V(lambda e, j=j, jt=jt: e.tensor_copy(
                        wrow[j][:, 128 * jt:128 * (jt + 1)],
                        psa[0:1, 1536 + 128 * j:1536 + 128 * (j + 1)]),
                      waits=[(tc_, tc_.v)], inc=vc)
            rows_done = (vc, vc.v)
            for j in range(2):
                for ch in range(TH // 512):
                    T(lambda e, j=j, ch=ch: e.matmul(
                        bankB(0), ones1f, wrow[j][0:1, 512 * ch:512 * (ch + 1)],
                        start=True, stop=True),
                      waits=[rows_done, (vc, vc.v)], inc=tc_)
                    V(lambda e, j=j, ch=ch: e.tensor_copy(
                        wexp[j][:, 512 * ch:512 * (ch + 1)], bankB(0)),
                      waits=[(tc_, tc_.v)], inc=vc)
                    T(lambda e, j=j, ch=ch: e.matmul(
                        bankB(1), e8, mrow[j][0:8, 512 * ch:512 * (ch + 1)],
                        start=True, stop=True),
                      waits=[(vc, vc.v)], inc=tc_)
                    V(lambda e, j=j, ch=ch: e.tensor_copy(
                        mex[j][:, 512 * ch:512 * (ch + 1)], bankB(1)),
                      waits=[(tc_, tc_.v)], inc=vc)
            bc_done = (vc, vc.v)

            for br, a_sb in enumerate((a1t, a3t)):
                for ch in range(TH // 512):
                    tb = ch % 2
                    for c in range(ND):
                        T(lambda e, c=c, ch=ch, tb=tb, a_sb=a_sb: e.matmul(
                            bankA(tb), a_sb[:, 128 * c:128 * (c + 1)],
                            nxh[:, TH * c + 512 * ch:TH * c + 512 * (ch + 1)],
                            start=(c == 0), stop=(c == ND - 1)),
                          waits=[(vc, vc.v), consts_lg] if c == 0 else [],
                          inc=(tc_ if c == ND - 1 else None))
                    trv = tc_.v
                    for j in range(2):
                        V(lambda e, br=br, j=j, ch=ch, tb=tb: e.tensor_tensor(
                            trsel[br][j][:, 512 * ch:512 * (ch + 1)], bankA(tb),
                            mex[j][:, 512 * ch:512 * (ch + 1)], ALU.mult),
                          waits=[(tc_, trv), bc_done], inc=vc)
            trsel_done = (vc, vc.v)
            for j in range(2):
                V(lambda e, j=j: e.memset(rrs[j][:, :], 0.0), waits=[(tc_, tc_.v)], inc=vc)
            rr_zero = (vc, vc.v)

            w_rel = [(tc_, 0), (tc_, 0)]
            grel = [[], []]
            for fc in range(NFC):
                sl = fc % 2
                G(lambda e, sl=sl, fc=fc: e.dma_start(
                    out=wfsl[sl], in_=wfs_h[:, FSC * fc:FSC * (fc + 1)]),
                  waits=[w_rel[sl]], inc=w1l[sl], incn=16)
                sl_ld = [(w1l[sl], w1l[sl].v)]
                for tcc in range(n_tc):
                    tco = TC * tcc
                    G0, G1 = bankA, bankB
                    silu_v = [0, 0]
                    a2v = [0, 0]

                    def emit_h(j, grp, gwaits):
                        h1v = 0
                        for br in range(2):
                            wslb = (wsl1 if br == 0 else wsl3)[sl]
                            bslb = (b1sl if br == 0 else b3sl)[sl]
                            for m in range(KC):
                                bank_i = KC * br + m
                                for c in range(ND):
                                    T(lambda e, c=c, m=m, grp=grp, bank_i=bank_i,
                                        wslb=wslb, tco=tco: e.matmul(
                                        grp(bank_i),
                                        wslb[:, FC * c + 128 * m:FC * c + 128 * (m + 1)],
                                        nxh[:, TH * c + tco:TH * c + tco + TC],
                                        start=(c == 0), stop=False),
                                      waits=gwaits if (c == 0 and m == 0 and br == 0) else [])
                                T(lambda e, m=m, br=br, j=j, grp=grp, bank_i=bank_i,
                                    bslb=bslb, tco=tco: e.matmul(
                                    grp(bank_i),
                                    bslb[:, 128 * m:128 * (m + 1)],
                                    trsel[br][j][:, tco:tco + TC],
                                    start=False, stop=True),
                                  inc=(tc_ if m == KC - 1 else None))
                                if br == 0 and m == KC - 1:
                                    h1v = tc_.v
                        return h1v, tc_.v

                    def emit_evac(j, grp, h1v, hv):
                        for bi in range(KC):
                            SC(lambda e, bi=bi, grp=grp: e.activation(
                                silu_s[:, 512 * bi:512 * (bi + 1)], grp(bi), AF.Silu),
                               waits=[(tc_, h1v), (vc, vc.v)], inc=sc)
                        silu_v[j] = sc.v
                        for bi in range(KC):
                            V(lambda e, bi=bi, grp=grp, j=j, tco=tco: e.tensor_tensor(
                                h3w_s[:, 512 * bi:512 * (bi + 1)],
                                grp(KC + bi),
                                wexp[j][:, tco:tco + TC], ALU.mult),
                              waits=[(tc_, hv)])
                        V(lambda e, j=j: e.tensor_tensor(act2w[j], silu_s, h3w_s, ALU.mult),
                          waits=[(sc, silu_v[j])])
                        a2v[j] = vc.v

                    def emit_rr(j, rbank):
                        for c4 in range(KC):
                            T(lambda e, c4=c4, j=j, rbank=rbank, sl=sl: e.matmul(
                                rbank, a2sl[sl][:, 128 * c4:128 * (c4 + 1)],
                                act2w[j][:, TC * c4:TC * (c4 + 1)],
                                start=(c4 == 0), stop=(c4 == KC - 1)),
                              waits=([(vc, a2v[j]), (sc, silu_v[1]), (vc, vc.v)] + sl_ld)
                              if c4 == 0 else [],
                              inc=(tc_ if c4 == KC - 1 else None))
                        rrv = tc_.v
                        V(lambda e, j=j, rrv=rrv, rbank=rbank, tco=tco: e.tensor_tensor(
                            rrs[j][:, tco:tco + TC], rbank, rrs[j][:, tco:tco + TC], ALU.add),
                          waits=[(tc_, rrv), rr_zero])

                    def emit_w2_half(grp, gten, ddh):
                        for dd4 in range(4):
                            dd = 4 * ddh + dd4
                            for j in range(2):
                                for c4 in range(KC):
                                    T(lambda e, dd=dd, dd4=dd4, c4=c4, j=j, grp=grp,
                                        sl=sl: e.matmul(
                                        grp(dd4),
                                        wsl2[sl][:, D * c4 + 128 * dd:D * c4 + 128 * (dd + 1)],
                                        act2w[j][:, TC * c4:TC * (c4 + 1)],
                                        start=(j == 0 and c4 == 0), stop=(j == 1 and c4 == KC - 1)),
                                      waits=([(vc, a2v[1]), (sc, silu_v[1]), (vc, a2v[0]),
                                              (vc, vc.v)] + sl_ld)
                                      if (c4 == 0 and dd4 == 0 and j == 0) else [],
                                      inc=(tc_ if (j == 1 and c4 == KC - 1) else None))
                            wv = tc_.v
                            if fc == 0:
                                V(lambda e, dd=dd, dd4=dd4, grp=grp, tco=tco, wv=wv: e.tensor_copy(
                                    outb[:, TH * dd + tco:TH * dd + tco + TC], grp(dd4)),
                                  waits=[(tc_, wv)])
                            else:
                                V(lambda e, dd=dd, dd4=dd4, grp=grp, tco=tco, wv=wv: e.tensor_tensor(
                                    outb[:, TH * dd + tco:TH * dd + tco + TC], grp(dd4),
                                    outb[:, TH * dd + tco:TH * dd + tco + TC], ALU.add),
                                  waits=[(tc_, wv)])

                    h1v0, hv0 = emit_h(0, G0, sl_ld + [trsel_done] + grel[0])
                    emit_evac(0, G0, h1v0, hv0)
                    h1v1, hv1 = emit_h(1, G1, grel[1])
                    emit_evac(1, G1, h1v1, hv1)
                    emit_rr(0, G1(0, TC))
                    emit_w2_half(G0, psa, 0)
                    emit_rr(1, G1(1, TC))
                    emit_w2_half(G1, psb, 1)
                    grel[0] = [(vc, vc.v)]
                    grel[1] = [(vc, vc.v)]
                w_rel[sl] = (tc_, tc_.v)
            for j in range(2):
                V(lambda e, j=j: e.tensor_tensor(rrs[j], rrs[j], mex[j], ALU.mult),
                  waits=[(vc, vc.v)], inc=vc)
                V(lambda e, j=j: e.tensor_copy(rrb[j], rrs[j]), waits=[(vc, vc.v)], inc=vc)
            rbv = vc.v
            for tcc in range(TH // 512):
                tco = 512 * tcc
                for dd in range(ND):
                    bank_i = dd % 4
                    for j in range(2):
                        T(lambda e, j=j, dd=dd, tco=tco, bank_i=bank_i: e.matmul(
                            bankA(bank_i), b2c[:, 128 * dd:128 * (dd + 1)],
                            rrb[j][:, tco:tco + 512],
                            start=(j == 0), stop=(j == 1)),
                          waits=[(vc, rbv), (vc, vc.v), consts_lg] if j == 0 else [],
                          inc=(tc_ if j == 1 else None))
                    V(lambda e, dd=dd, tco=tco, bank_i=bank_i: e.tensor_tensor(
                        outb[:, TH * dd + tco:TH * dd + tco + 512],
                        bankA(bank_i),
                        outb[:, TH * dd + tco:TH * dd + tco + 512], ALU.add),
                      waits=[(tc_, tc_.v)], inc=vc)
            ar.pop()
            p.fence()
            ffn_done = (vc, vc.v)

            ar.push()
            sgh = ar.f32(128, ND * TH)
            stile = [ar.f32(128, TH) for _ in range(2)]
            gtile = ar.f32(128, TH)
            ofin = [ar.f32(128, D) for _ in range(2)]
            xrt = [ar.f32(128, D) for _ in range(2)]
            strel = [(vc, 0), (vc, 0)]
            for dd in range(ND):
                sl = dd % 2
                Y(lambda e, dd=dd, sl=sl, tho=tho: e.dma_start(
                    out=stile[sl], in_=ssp[dd, :, tho:tho + TH]),
                  waits=[strel[sl]] + s_spill_done, inc=stl[sl], incn=16)
                st_ld = (stl[sl], stl[sl].v)
                SC(lambda e, dd=dd: e.activation(gtile, outb[:, TH * dd:TH * (dd + 1)], AF.Silu),
                   waits=[ffn_done, (vc, vc.v)], inc=sc)
                V(lambda e, dd=dd, sl=sl: e.tensor_tensor(
                    sgh[:, TH * dd:TH * (dd + 1)], stile[sl], gtile, ALU.mult),
                  waits=[(sc, sc.v), st_ld], inc=vc)
                strel[sl] = (vc, vc.v)
            sg_done = (vc, vc.v)
            orel_f = [(ofs[0], 0), (ofs[1], 0)]
            for jt in range(TH // 128):
                osl = jt % 2
                Y(lambda e, jt=jt, osl=osl, tho=tho, b=b: e.dma_start(
                    out=xrt[osl], in_=x_h[b, tho + 128 * jt:tho + 128 * (jt + 1), :]),
                  waits=[orel_f[osl]], inc=xfl[osl], incn=16)
                xf_ld = (xfl[osl], xfl[osl].v)
                for dd in range(ND):
                    tb = dd % 2
                    T(lambda e, jt=jt, dd=dd, tb=tb: e.transpose(
                        bankB(tb, 128), sgh[:, TH * dd + 128 * jt:TH * dd + 128 * (jt + 1)], ident),
                      waits=[sg_done, (vc, vc.v)], inc=tc_)
                    V(lambda e, jt=jt, dd=dd, osl=osl, tb=tb: e.tensor_tensor(
                        ofin[osl][:, 128 * dd:128 * (dd + 1)], bankB(tb, 128),
                        xrt[osl][:, 128 * dd:128 * (dd + 1)], ALU.add),
                      waits=[(tc_, tc_.v), xf_ld], inc=vc)
                Y(lambda e, jt=jt, osl=osl, tho=tho, b=b: e.dma_start(
                    out=out_h[b, tho + 128 * jt:tho + 128 * (jt + 1), :], in_=ofin[osl]),
                  waits=[(vc, vc.v)], inc=ofs[osl], incn=16)
                orel_f[osl] = (ofs[osl], ofs[osl].v)
            ar.pop()
            p.fence()
        ar.pop()

    final_sems = [(ofs[0], ofs[0].v), (ofs[1], ofs[1].v),
                  (ssl_[0], ssl_[0].v), (ssl_[1], ssl_[1].v)] +                  [(stsp[c], stsp[c].v) for c in range(ND)]
    p.op("sync", lambda e: None, waits=final_sems)
    p.finalize()
    return nc


def prep_weights(i):
    def bf(x):
        return np.ascontiguousarray(x, dtype=np.float32)

    w = {}
    for nm, W in (("wq", i["L0_attn_Wq"]), ("wk", i["L0_attn_Wk"]),
                  ("wv", i["L0_attn_Wv"]), ("wo", i["L0_attn_Wo"])):
        w[nm] = bf(np.asarray(W).reshape(8, 128, 8, 128).transpose(1, 2, 0, 3).reshape(128, 8192))
    w["bias"] = bf(np.concatenate(
        [np.asarray(i["L0_attn_b" + k]).reshape(1, D) for k in "qkv"]
        + [np.asarray(i["L0_attn_bo"]).reshape(1, D)], axis=1))
    lnw = np.asarray(i["L0_attn_ln_w"]).reshape(8, 128).T
    lnb = np.asarray(i["L0_attn_ln_b"]).reshape(8, 128).T
    nwc = np.asarray(i["L0_ffn_norm_w"]).reshape(8, 128).T
    w["lncol"] = bf(np.concatenate([lnw, lnb, nwc], axis=1))
    w1t = np.asarray(i["L0_ffn_w1"]).T
    w3t = np.asarray(i["L0_ffn_w3"]).T
    w["w1t"] = bf(w1t.reshape(8, 128, NFC, FC).transpose(1, 2, 0, 3).reshape(128, NFC * 8 * FC))
    w["w3t"] = bf(w3t.reshape(8, 128, NFC, FC).transpose(1, 2, 0, 3).reshape(128, NFC * 8 * FC))
    w2t = np.asarray(i["L0_ffn_w2"]).T
    w["w2t"] = bf(w2t.reshape(NFC, KC, 128, D).transpose(2, 0, 1, 3).reshape(128, NFC * KC * D))
    w["b1c"] = bf(np.asarray(i["L0_ffn_b1"]).transpose(0, 2, 1).reshape(ER, FF))
    w["b3c"] = bf(np.asarray(i["L0_ffn_b3"]).transpose(0, 2, 1).reshape(ER, FF))
    a2t = np.asarray(i["L0_ffn_a2"]).reshape(ER, FF).T
    w["a2t"] = bf(a2t.reshape(NFC, KC, 128, ER).transpose(2, 0, 1, 3).reshape(128, NFC * KC * ER))
    a1t = np.asarray(i["L0_ffn_a1"]).reshape(ER, D).T
    a3t = np.asarray(i["L0_ffn_a3"]).reshape(ER, D).T
    w["a1t"] = bf(a1t.reshape(8, 128, 128).transpose(1, 0, 2).reshape(128, 8 * 128))
    w["a3t"] = bf(a3t.reshape(8, 128, 128).transpose(1, 0, 2).reshape(128, 8 * 128))
    w["b2c"] = bf(np.asarray(i["L0_ffn_b2"]).transpose(0, 2, 1).reshape(ER, D))
    FSC = 8 * FC + 8 * FC + KC * D + FC + FC + KC * 128
    fs = np.empty((128, NFC * FSC), np.float32)
    for fc in range(NFC):
        o = fc * FSC
        fs[:, o:o + 8 * FC] = w["w1t"][:, 8 * FC * fc:8 * FC * (fc + 1)]
        o += 8 * FC
        fs[:, o:o + 8 * FC] = w["w3t"][:, 8 * FC * fc:8 * FC * (fc + 1)]
        o += 8 * FC
        fs[:, o:o + KC * D] = w["w2t"][:, KC * D * fc:KC * D * (fc + 1)]
        o += KC * D
        fs[:, o:o + FC] = w["b1c"][:, FC * fc:FC * (fc + 1)]
        o += FC
        fs[:, o:o + FC] = w["b3c"][:, FC * fc:FC * (fc + 1)]
        o += FC
        fs[:, o:o + KC * 128] = w["a2t"][:, KC * 128 * fc:KC * 128 * (fc + 1)]
    w["wfs"] = np.ascontiguousarray(fs)
    gwt = np.asarray(i["L0_ffn_gate_w"]).T
    w["gw"] = bf(gwt.reshape(8, 128, 8).transpose(1, 0, 2).reshape(128, 64))
    e2m = np.zeros((2, 128), np.float32)
    e2m[0, 0:64] = 1.0
    e2m[1, 64:128] = 1.0
    w["e2"] = e2m
    e8m = np.zeros((8, 128), np.float32)
    for ee in range(8):
        e8m[ee, 16 * ee:16 * (ee + 1)] = 2.0
    w["e8"] = e8m
    w["ident"] = np.eye(128, dtype=np.float32)
    return w


_CACHE = {}


def kernel(**inputs):
    S, NB, NCORE = 2048, 2, 8
    key = (S, NB)
    if key not in _CACHE:
        _CACHE[key] = build_program(S, NB)
    nc = _CACHE[key]
    w = prep_weights(inputs)
    x = np.ascontiguousarray(np.asarray(inputs["L0_attn_hidden_states"], dtype=np.float32))
    in_maps = []
    for cid in range(NCORE):
        m = dict(w)
        m["x"] = np.ascontiguousarray(x[NB * cid:NB * (cid + 1)])
        in_maps.append(m)
    res = run_bass_kernel_spmd(nc, in_maps, core_ids=list(range(NCORE)))
    out = np.concatenate([np.asarray(r["out"], dtype=np.float32) for r in res.results], axis=0)
    return out


# version 19
# speedup vs baseline: 1.0188x; 1.0188x over previous
import numpy as np
from collections import defaultdict

import concourse.bass as bass
import concourse.mybir as mybir
from concourse.bass_utils import run_bass_kernel_spmd

F32 = mybir.dt.float32
BF16 = mybir.dt.bfloat16
U8 = mybir.dt.uint8
AF = mybir.ActivationFunctionType
ALU = mybir.AluOpType
AX = mybir.AxisListType

D = 1024
ND = 8
FF = 4096
HD = 64
NHEAD = 16
NPAIR = 8
ER = 128
FC = 256
KC = FC // 128
NFC = FF // FC
TC = 512


class Ctr:
    __slots__ = ("sem", "v", "name")

    def __init__(self, nc, name):
        self.sem = nc.alloc_semaphore(name)
        self.v = 0
        self.name = name


class Prog:
    ENGINES = ("sync", "gpsimd", "tensor", "vector", "scalar")

    def __init__(self, nc, dedup=True):
        self.nc = nc
        self.dedup = dedup
        self.streams = defaultdict(list)
        self._waited = defaultdict(dict)
        self.fctr = {}
        self.dma_ctrs_sync = []
        self.dma_ctrs_gp = []
        self.own_ctr = {}

    def ctr(self, name):
        return Ctr(self.nc, name)

    def op(self, eng, fn, waits=(), inc=None, incn=1, noinc=False):
        own = self.own_ctr.get(eng)
        wlist = []
        if own is not None and own.v > 0:
            wlist.append((own, own.v))
        for w in waits:
            c, v = w if isinstance(w, tuple) else (w, w.v)
            if v > 0:
                wlist.append((c, v))
        if own is not None and not noinc:
            assert inc is None or inc is own
            inc = own
        if inc is not None:
            inc.v += incn

        def run(engine, _w=wlist, _inc=inc, _incn=incn, _fn=fn, _eng=eng):
            seen = self._waited[_eng]
            for c, v in _w:
                if self.dedup and seen.get(c.name, 0) >= v:
                    continue
                engine.wait_ge(c.sem, v)
                seen[c.name] = v
            inst = _fn(engine)
            if _inc is not None:
                inst.then_inc(_inc.sem, _incn)
            return inst

        self.streams[eng].append(run)

    def fence(self, engines=None):
        engines = engines or self.ENGINES
        for e in engines:
            if e not in self.fctr:
                self.fctr[e] = self.ctr("f_" + e)
        for e, ctrs in (("sync", self.dma_ctrs_sync), ("gpsimd", self.dma_ctrs_gp)):
            if e in engines:
                self.op(e, lambda eng: None, waits=[(c, c.v) for c in ctrs], noinc=True)
        for e in engines:
            if e in self.own_ctr:
                self.op(e, lambda eng: eng.nop())
            else:
                self.op(e, lambda eng: eng.nop(), inc=self.fctr[e])
        snaps = []
        for e in engines:
            c = self.own_ctr.get(e) or self.fctr[e]
            snaps.append((e, c, c.v))
        for e in engines:
            ws = [(c, v) for (e2, c, v) in snaps if e2 != e]
            self.op(e, lambda eng: None, waits=ws, noinc=True)

    def finalize(self):
        with self.nc.Block() as block:
            for eng in self.ENGINES:
                fns = self.streams.get(eng)
                if not fns:
                    continue

                def make(fns=fns):
                    def body(engine):
                        for f in fns:
                            f(engine)
                    return body

                getattr(block, eng)(make())


class Arena:
    def __init__(self, ap, nbytes):
        self.base = ap
        self.n = nbytes
        self.off = 0
        self.stack = []
        self.peak = 0

    def push(self):
        self.stack.append(self.off)

    def pop(self):
        self.off = self.stack.pop()

    def _al(self, parts, cols, esz, dt):
        self.off = (self.off + 31) & ~31
        nb = cols * esz
        v = self.base[:, self.off:self.off + nb].bitcast(dt)
        self.off += nb
        self.peak = max(self.peak, self.off)
        assert self.off <= self.n, f"arena overflow {self.off} > {self.n}"
        if parts != 128:
            v = v[0:parts, :]
        return v

    def f32(self, parts, cols):
        return self._al(parts, cols, 4, F32)

    def bf16(self, parts, cols):
        return self._al(parts, cols, 2, BF16)


def build_program(S=2048, NB=2, dedup=True):
    n_tt = S // 128
    HL = min(S, 1024)
    NQH = S // HL
    TH = min(S, 1024)
    NTH = S // TH
    n_tc = TH // TC
    n_qc = S // 512

    nc = bass.Bass("TRN2", target_bir_lowering=False)

    x_h = nc.dram_tensor("x", [NB, S, D], F32, kind="ExternalInput").ap()
    out_h = nc.dram_tensor("out", [NB, S, D], F32, kind="ExternalOutput").ap()

    def win(name, shape):
        return nc.dram_tensor(name, shape, F32, kind="ExternalInput").ap()

    wq_h = win("wq", [128, 8 * D])
    wk_h = win("wk", [128, 8 * D])
    wv_h = win("wv", [128, 8 * D])
    wo_h = win("wo", [128, 8 * D])
    bias_h = win("bias", [1, 4 * D])
    lncol_h = win("lncol", [128, 24])
    w1t_h = win("w1t", [128, NFC * 8 * FC])
    w3t_h = win("w3t", [128, NFC * 8 * FC])
    w2t_h = win("w2t", [128, NFC * KC * D])
    b1c_h = win("b1c", [128, FF])
    b3c_h = win("b3c", [128, FF])
    a2t_h = win("a2t", [128, NFC * KC * 128])
    FSC = 8 * FC + 8 * FC + KC * D + FC + FC + KC * 128
    wfs_h = win("wfs", [128, NFC * FSC])
    a1t_h = win("a1t", [128, ND * 128])
    a3t_h = win("a3t", [128, ND * 128])
    b2c_h = win("b2c", [128, ND * 128])
    gw_h = win("gw", [128, ND * 8])
    e2_h = win("e2", [2, 128])
    e8_h = win("e8", [8, 128])
    id_h = win("ident", [128, 128])

    xnsp = nc.dram_tensor("xnsp", [ND, 128, S], F32, kind="Internal").ap()
    nxsp = nc.dram_tensor("nxsp", [ND, 128, S], BF16, kind="Internal").ap()
    ssp = nc.dram_tensor("ssp", [ND, 128, S], F32, kind="Internal").ap()

    ones_row = nc.alloc_sbuf_tensor("ones_row", [1, 512], BF16).ap()
    ones1f = nc.alloc_sbuf_tensor("ones1f", [1, 128], F32).ap()
    onescf = nc.alloc_sbuf_tensor("onescf", [128, 64], F32).ap()
    ones64b = nc.alloc_sbuf_tensor("ones64b", [64, 1], BF16).ap()
    ind2 = nc.alloc_sbuf_tensor("ind2", [128, 2], BF16).ap()
    e2 = nc.alloc_sbuf_tensor("e2s", [2, 128], F32).ap()
    e8 = nc.alloc_sbuf_tensor("e8s", [8, 128], F32).ap()
    ident = nc.alloc_sbuf_tensor("idents", [128, 128], F32).ap()
    biasr = nc.alloc_sbuf_tensor("biasrs", [1, 4 * D], BF16).ap()
    lncol = nc.alloc_sbuf_tensor("lncols", [128, 24], F32).ap()
    gw = nc.alloc_sbuf_tensor("gws", [128, ND * 8], BF16).ap()
    a1t = nc.alloc_sbuf_tensor("a1ts", [128, ND * 128], BF16).ap()
    a3t = nc.alloc_sbuf_tensor("a3ts", [128, ND * 128], BF16).ap()
    b2c = nc.alloc_sbuf_tensor("b2cs", [128, ND * 128], BF16).ap()
    statc = nc.alloc_sbuf_tensor("statc", [128, 16 * 8], F32).ap()

    psa = nc.alloc_psum_tensor("psa", [128, 2048], F32).ap()
    psb = nc.alloc_psum_tensor("psb", [128, 2048], F32).ap()

    def bankA(i, cols=512):
        return psa[:, 512 * i:512 * i + cols]

    def bankB(i, cols=512):
        return psb[:, 512 * i:512 * i + cols]

    rem = nc._state.sbuf_bytes_remaining - 1024
    arena_t = nc.alloc_sbuf_tensor("arena", [128, rem], U8).ap()
    ar = Arena(arena_t, rem)

    p = Prog(nc, dedup=dedup)
    tc_ = p.ctr("tc")
    vc = p.ctr("vc")
    sc = p.ctr("sc")
    c_ld = p.ctr("c_ld")
    c_lg = p.ctr("c_lg")
    xld = [p.ctr("xld0"), p.ctr("xld1")]
    stsp = [p.ctr(f"stsp{c}") for c in range(ND)]
    wql = [p.ctr("wql0"), p.ctr("wql1")]
    wkl = [p.ctr("wkl0"), p.ctr("wkl1")]
    wvl = [p.ctr("wvl0"), p.ctr("wvl1")]
    wol = [p.ctr("wol0"), p.ctr("wol1")]
    xnl = [p.ctr("xnl0"), p.ctr("xnl1")]
    ssl_ = [p.ctr("ssl0"), p.ctr("ssl1")]
    nxl = p.ctr("nxl")
    w1l = [p.ctr("w1l0"), p.ctr("w1l1")]
    w3l = [p.ctr("w3l0"), p.ctr("w3l1")]
    w2l = [p.ctr("w2l0"), p.ctr("w2l1")]
    bl1 = [p.ctr("bl10"), p.ctr("bl11")]
    bl3 = [p.ctr("bl30"), p.ctr("bl31")]
    a2l = [p.ctr("a2l0"), p.ctr("a2l1")]
    stl = [p.ctr("stl0"), p.ctr("stl1")]
    xfl = [p.ctr("xfl0"), p.ctr("xfl1")]
    ofs = [p.ctr("ofs0"), p.ctr("ofs1")]
    kcl = [p.ctr("kcl0"), p.ctr("kcl1")]
    qcl = [p.ctr("qcl0"), p.ctr("qcl1")]
    p.own_ctr = {"vector": vc, "scalar": sc}
    p.dma_ctrs_sync = [c_ld] + xld + stsp + xnl + ssl_ + [nxl] + stl + xfl + ofs + kcl
    p.dma_ctrs_gp = [c_lg] + wql + wkl + wvl + wol + w1l + w3l + w2l + bl1 + bl3 + a2l + qcl

    def T(fn, waits=(), inc=None, incn=1):
        p.op("tensor", fn, waits, inc, incn)

    def V(fn, waits=(), inc=None, incn=1):
        p.op("vector", fn, waits, inc, incn)

    def SC(fn, waits=(), inc=None, incn=1):
        p.op("scalar", fn, waits, inc, incn)

    def Y(fn, waits=(), inc=None, incn=1):
        p.op("sync", fn, waits, inc, incn)

    def G(fn, waits=(), inc=None, incn=1):
        p.op("gpsimd", fn, waits, inc, incn)

    V(lambda e: e.memset(ones_row[:, :], 1.0), inc=vc)
    V(lambda e: e.memset(ones64b[:, :], 1.0), inc=vc)
    V(lambda e: e.memset(ones1f[:, :], 1.0), inc=vc)
    V(lambda e: e.memset(onescf[:, :], 1.0), inc=vc)
    V(lambda e: e.memset(ind2[:, :], 0.0), inc=vc)
    V(lambda e: e.memset(ind2[0:64, 0:1], 1.0), waits=[vc], inc=vc)
    V(lambda e: e.memset(ind2[64:128, 1:2], 1.0), waits=[vc], inc=vc)
    Y(lambda e: e.dma_start(out=e2, in_=e2_h), inc=c_ld, incn=16)
    Y(lambda e: e.dma_start(out=e8, in_=e8_h), inc=c_ld, incn=16)
    Y(lambda e: e.dma_start(out=ident, in_=id_h), inc=c_ld, incn=16)
    Y(lambda e: e.dma_start(out=lncol, in_=lncol_h), inc=c_ld, incn=16)
    G(lambda e: e.dma_start(out=biasr, in_=bias_h), inc=c_lg, incn=16)
    G(lambda e: e.dma_start(out=gw, in_=gw_h), inc=c_lg, incn=16)
    G(lambda e: e.dma_start(out=a1t, in_=a1t_h), inc=c_lg, incn=16)
    G(lambda e: e.dma_start(out=a3t, in_=a3t_h), inc=c_lg, incn=16)
    G(lambda e: e.dma_start(out=b2c, in_=b2c_h), inc=c_lg, incn=16)
    consts_v = vc.v
    consts_ld = (c_ld, c_ld.v)
    consts_lg = (c_lg, c_lg.v)

    for b in range(NB):
        ar.push()
        x2T = ar.bf16(128, ND * S)
        qbuf = ar.bf16(128, ND * S)
        kbuf = ar.bf16(128, ND * S)
        vbuf = ar.bf16(128, n_tt * NHEAD * 65 + 63)

        ar.push()
        xt = [ar.f32(128, D) for _ in range(2)]
        sqf = ar.f32(128, D)
        xnb = ar.f32(128, D)
        x2p = ar.f32(128, D)
        nxp = ar.f32(128, D)
        xnstg = [ar.f32(128, 512) for _ in range(ND)]
        nxstg = [ar.bf16(128, 512) for _ in range(ND)]

        xrel = [(vc, 0), (vc, 0)]
        for i in range(n_tt):
            sl = i % 2
            Y(lambda e, sl=sl, i=i, b=b: e.dma_start(out=xt[sl], in_=x_h[b, 128 * i:128 * (i + 1), :]),
              waits=[xrel[sl]], inc=xld[sl], incn=16)
            ldw = (xld[sl], xld[sl].v)
            SC(lambda e, sl=sl: e.activation(sqf, xt[sl], AF.Square),
               waits=[ldw, (vc, vc.v)], inc=sc)
            V(lambda e, sl=sl, i=i: e.reduce_sum(statc[:, i:i + 1], xt[sl], axis=AX.X),
              waits=[ldw], inc=vc)
            V(lambda e, i=i: e.reduce_sum(statc[:, 16 + i:17 + i], sqf, axis=AX.X),
              waits=[(sc, sc.v)], inc=vc)
            xrel[sl] = (vc, vc.v)

        cs = statc

        def col(k):
            return cs[:, 16 * k:16 * k + n_tt]

        V(lambda e: e.tensor_scalar(col(2), col(0), 1.0 / D, None, ALU.mult), waits=[vc], inc=vc)
        V(lambda e: e.tensor_tensor(col(7), col(2), col(2), ALU.mult), waits=[vc], inc=vc)
        V(lambda e: e.tensor_scalar(col(3), col(1), 1.0 / D, None, ALU.mult), waits=[vc], inc=vc)
        V(lambda e: e.tensor_tensor(col(3), col(3), col(7), ALU.subtract), waits=[vc], inc=vc)
        V(lambda e: e.tensor_scalar(col(7), col(3), 1e-5, None, ALU.add), waits=[vc], inc=vc)
        SC(lambda e: e.activation(col(4), col(7), AF.Ln), waits=[vc], inc=sc)
        SC(lambda e: e.activation(col(4), col(4), AF.Exp, scale=-0.5), waits=[sc], inc=sc)
        V(lambda e: e.reciprocal(col(5), col(7)), waits=[vc, sc], inc=vc)
        V(lambda e: e.tensor_tensor(col(5), col(3), col(5), ALU.mult), waits=[vc], inc=vc)
        V(lambda e: e.tensor_scalar(col(6), col(5), 1e-6, None, ALU.add), waits=[vc], inc=vc)
        SC(lambda e: e.activation(col(6), col(6), AF.Ln), waits=[vc], inc=sc)
        SC(lambda e: e.activation(col(6), col(6), AF.Exp, scale=-0.5), waits=[sc], inc=sc)
        V(lambda e: e.tensor_scalar(col(5), col(5), 1e-5, None, ALU.add), waits=[sc], inc=vc)
        SC(lambda e: e.activation(col(5), col(5), AF.Ln), waits=[vc], inc=sc)
        SC(lambda e: e.activation(col(5), col(5), AF.Exp, scale=-0.5), waits=[sc], inc=sc)
        V(lambda e: e.tensor_tensor(col(5), col(4), col(5), ALU.mult), waits=[sc], inc=vc)
        V(lambda e: e.tensor_tensor(col(6), col(4), col(6), ALU.mult), waits=[vc], inc=vc)
        stats_done = (vc, vc.v)

        stg_rel = [(stsp[c], 0) for c in range(ND)]
        xrelA = list(xrel)
        trA_rel = [(sc, 0), (sc, 0), (vc, 0), (vc, 0)]
        trB_rel = [(vc, 0), (vc, 0)]
        for i in range(n_tt):
            sl = i % 2
            Y(lambda e, sl=sl, i=i, b=b: e.dma_start(out=xt[sl], in_=x_h[b, 128 * i:128 * (i + 1), :]),
              waits=[xrelA[sl]], inc=xld[sl], incn=16)
            ldw = (xld[sl], xld[sl].v)
            V(lambda e, sl=sl, i=i: e.tensor_scalar(
                xnb, xt[sl], cs[:, 32 + i:33 + i], cs[:, 64 + i:65 + i], ALU.subtract, ALU.mult),
              waits=[ldw, stats_done, (tc_, tc_.v)], inc=vc)
            V(lambda e, sl=sl, i=i: e.tensor_scalar(
                x2p, xt[sl], cs[:, 32 + i:33 + i], cs[:, 80 + i:81 + i], ALU.subtract, ALU.mult),
              waits=[(tc_, tc_.v)], inc=vc)
            V(lambda e, sl=sl, i=i: e.tensor_scalar(
                nxp, xt[sl], cs[:, 32 + i:33 + i], cs[:, 96 + i:97 + i], ALU.subtract, ALU.mult),
              waits=[(tc_, tc_.v)], inc=vc)
            xrelA[sl] = (vc, vc.v)
            apv = vc.v
            for c in range(ND):
                tb = c % 2
                T(lambda e, c=c, tb=tb: e.transpose(
                    bankA(tb, 128), xnb[:, 128 * c:128 * (c + 1)], ident),
                  waits=[(vc, apv), trA_rel[tb], consts_ld], inc=tc_)
                t1 = tc_.v
                T(lambda e, c=c, tb=tb: e.transpose(
                    bankA(2 + tb, 128), x2p[:, 128 * c:128 * (c + 1)], ident),
                  waits=[trA_rel[2 + tb]], inc=tc_)
                t2 = tc_.v
                T(lambda e, c=c, tb=tb: e.transpose(
                    bankB(tb, 128), nxp[:, 128 * c:128 * (c + 1)], ident),
                  waits=[trB_rel[tb]], inc=tc_)
                t3 = tc_.v
                SC(lambda e, c=c, i=i, tb=tb: e.activation(
                    xnstg[c][:, 128 * (i % 4):128 * (i % 4 + 1)], bankA(tb, 128), AF.Copy),
                   waits=[(tc_, t1), stg_rel[c]], inc=sc)
                trA_rel[tb] = (sc, sc.v)
                V(lambda e, c=c, i=i, tb=tb: e.tensor_scalar(
                    x2T[:, S * c + 128 * i: S * c + 128 * (i + 1)], bankA(2 + tb, 128),
                    lncol[:, c:c + 1], lncol[:, 8 + c:9 + c], ALU.mult, ALU.add),
                  waits=[(tc_, t2)], inc=vc)
                trA_rel[2 + tb] = (vc, vc.v)
                V(lambda e, c=c, i=i, tb=tb: e.tensor_scalar(
                    nxstg[c][:, 128 * (i % 4):128 * (i % 4 + 1)], bankB(tb, 128),
                    lncol[:, 16 + c:17 + c], None, ALU.mult),
                  waits=[(tc_, t3), stg_rel[c]], inc=vc)
                trB_rel[tb] = (vc, vc.v)
            if i % 4 == 3:
                for c in range(ND):
                    Y(lambda e, c=c, i=i: e.dma_start(
                        out=xnsp[c, :, 512 * (i // 4):512 * (i // 4) + 512], in_=xnstg[c]),
                      waits=[(sc, sc.v), (vc, vc.v)], inc=stsp[c], incn=16)
                    Y(lambda e, c=c, i=i: e.dma_start(
                        out=nxsp[c, :, 512 * (i // 4):512 * (i // 4) + 512], in_=nxstg[c]),
                      waits=[(vc, vc.v)], inc=stsp[c], incn=16)
                    stg_rel[c] = (stsp[c], stsp[c].v)
        a_spill_done = [(stsp[c], stsp[c].v) for c in range(ND)]
        x2t_done = (vc, vc.v)
        ar.pop()
        p.fence()

        ar.push()
        wsl = [[ar.bf16(128, D) for _ in range(2)] for _ in range(3)]
        qraw = [ar.f32(128, 512) for _ in range(2)]
        kraw = [ar.f32(128, 512) for _ in range(2)]
        rowa = ar.f32(2, 512)
        rowb_ = ar.f32(2, 512)
        sqs = ar.bf16(128, 512)
        wrel = [[(tc_, 0)] * 2 for _ in range(3)]
        bstate = {"pq": [[], []], "pk": [[], []], "pv": [[], []], "gqc": [0]}
        btail_pending = [None]
        for i in range(n_tt):
            V(lambda e, i=i: e.memset(vbuf[:, 1040 * i:1040 * (i + 1)], 1.0),
              waits=[(tc_, tc_.v)], inc=vc)
        V(lambda e: e.memset(vbuf[:, 1040 * n_tt:1040 * n_tt + 63], 1.0),
          waits=[(tc_, tc_.v)], inc=vc)
        vbuf_init = (vc, vc.v)

        for h in range(NPAIR):
            sl = h % 2
            wsems = (wql, wkl, wvl)
            for wi, wh in enumerate((wq_h, wk_h, wv_h)):
                G(lambda e, sl=sl, wi=wi, wh=wh, h=h: e.dma_start(
                    out=wsl[wi][sl], in_=wh[:, D * h:D * (h + 1)]),
                  waits=[wrel[wi][sl]], inc=wsems[wi][sl], incn=16)
            wq_ld = (wql[sl], wql[sl].v)
            wk_ld = (wkl[sl], wkl[sl].v)
            wv_ld = (wvl[sl], wvl[sl].v)
            bq_lh = biasr[0:1, 128 * h:128 * (h + 1)]
            bk_lh = biasr[0:1, D + 128 * h:D + 128 * (h + 1)]
            bv_rh = biasr[0:1, 2 * D + 128 * h:2 * D + 128 * (h + 1)]
            pq_rel = bstate["pq"]
            pk_rel = bstate["pk"]
            pv_rel = bstate["pv"]

            def flush_btail(info):
                h3, bqv_, qc2, qsl2, vsl2, q_ready2, k_ready2, tv2, cols2 = info
                for tt4 in range(4):
                    i2 = 4 * qc2 + tt4
                    for hh2 in range(2):
                        V(lambda e, i2=i2, tt4=tt4, vsl2=vsl2, hh2=hh2, h=h3: e.tensor_copy(
                            vbuf[:, 1040 * i2 + 65 * (2 * h + hh2):1040 * i2 + 65 * (2 * h + hh2) + 64],
                            psb[:, 512 * vsl2 + 128 * tt4 + 64 * hh2:512 * vsl2 + 128 * tt4 + 64 * hh2 + 64]),
                          waits=[(tc_, tv2), vbuf_init])
                pv_rel[vsl2] = [(vc, vc.v)]
                V(lambda e, qsl2=qsl2: e.tensor_copy(qraw[qsl2], bankA(qsl2)),
                  waits=[(tc_, q_ready2)])
                qcopy_v = vc.v
                SC(lambda e, qsl2=qsl2: e.activation(sqs, bankA(qsl2), AF.Square),
                   waits=[(tc_, q_ready2), (tc_, tc_.v), (vc, vc.v)], inc=sc)
                T(lambda e: e.matmul(psb[0:2, 1024:1536], ind2, sqs, start=True, stop=True),
                  waits=[(sc, sc.v), (vc, vc.v)], inc=tc_)
                V(lambda e: e.tensor_scalar(rowa, psb[0:2, 1024:1536], 1e-24, None, ALU.max),
                  waits=[(tc_, tc_.v), (sc, sc.v)])
                SC(lambda e: e.activation(rowb_, rowa, AF.Ln), waits=[(vc, vc.v)], inc=sc)
                SC(lambda e: e.activation(rowa, rowb_, AF.Exp, scale=-0.5), inc=sc)
                T(lambda e: e.matmul(bankB(2), e2, rowa[0:2, :], start=True, stop=True),
                  waits=[(sc, sc.v), (vc, vc.v), consts_ld], inc=tc_)
                V(lambda e, h=h3, cols2=cols2, qsl2=qsl2: e.tensor_tensor(
                    qbuf[:, S * h + cols2.start:S * h + cols2.stop], qraw[qsl2], bankB(2), ALU.mult),
                  waits=[(tc_, tc_.v)])
                pq_rel[qsl2] = [(sc, sc.v), (vc, qcopy_v)]
                V(lambda e, qsl2=qsl2: e.tensor_copy(kraw[qsl2], bankA(2 + qsl2)),
                  waits=[(tc_, k_ready2)])
                kcopy_v = vc.v
                SC(lambda e, qsl2=qsl2: e.activation(sqs, bankA(2 + qsl2), AF.Square),
                   waits=[(tc_, k_ready2), (vc, vc.v), (tc_, tc_.v)], inc=sc)
                T(lambda e: e.matmul(psb[0:2, 1536:2048], ind2, sqs, start=True, stop=True),
                  waits=[(sc, sc.v)], inc=tc_)
                V(lambda e: e.tensor_scalar(rowa, psb[0:2, 1536:2048], 1e-24, None, ALU.max),
                  waits=[(tc_, tc_.v), (sc, sc.v)])
                SC(lambda e: e.activation(rowb_, rowa, AF.Ln), waits=[(vc, vc.v)], inc=sc)
                SC(lambda e: e.activation(rowa, rowb_, AF.Exp, scale=-0.5), inc=sc)
                T(lambda e: e.matmul(bankB(3), e2, rowa[0:2, :], start=True, stop=True),
                  waits=[(sc, sc.v), (vc, vc.v)], inc=tc_)
                V(lambda e, h=h3, cols2=cols2, qsl2=qsl2: e.tensor_tensor(
                    kbuf[:, S * h + cols2.start:S * h + cols2.stop], kraw[qsl2], bankB(3), ALU.mult),
                  waits=[(tc_, tc_.v)])
                pk_rel[qsl2] = [(sc, sc.v), (vc, kcopy_v)]

            for qc in range(n_qc):
                qsl = bstate["gqc"][0] % 2
                bstate["gqc"][0] += 1
                cols = slice(512 * qc, 512 * (qc + 1))
                for c in range(ND):
                    T(lambda e, c=c, qsl=qsl, cols=cols, sl=sl: e.matmul(
                        bankA(qsl), wsl[0][sl][:, 128 * c:128 * (c + 1)],
                        x2T[:, S * c + cols.start:S * c + cols.stop],
                        start=(c == 0), stop=False),
                      waits=([wq_ld, x2t_done, (vc, consts_v), consts_lg] + pq_rel[qsl]) if c == 0 else [])
                T(lambda e, qsl=qsl, bq_lh=bq_lh: e.matmul(
                    bankA(qsl), bq_lh, ones_row, start=False, stop=True), inc=tc_)
                q_ready = tc_.v
                for c in range(ND):
                    T(lambda e, c=c, qsl=qsl, cols=cols, sl=sl: e.matmul(
                        bankA(2 + qsl), wsl[1][sl][:, 128 * c:128 * (c + 1)],
                        x2T[:, S * c + cols.start:S * c + cols.stop],
                        start=(c == 0), stop=False),
                      waits=([wk_ld] + pk_rel[qsl]) if c == 0 else [])
                T(lambda e, qsl=qsl, bk_lh=bk_lh: e.matmul(
                    bankA(2 + qsl), bk_lh, ones_row, start=False, stop=True), inc=tc_)
                k_ready = tc_.v
                vsl = qsl
                for tt4 in range(4):
                    i = 4 * qc + tt4
                    for c in range(ND):
                        T(lambda e, c=c, i=i, tt4=tt4, vsl=vsl, sl=sl: e.matmul(
                            psb[:, 512 * vsl + 128 * tt4:512 * vsl + 128 * (tt4 + 1)],
                            x2T[:, S * c + 128 * i:S * c + 128 * (i + 1)],
                            wsl[2][sl][:, 128 * c:128 * (c + 1)],
                            start=(c == 0), stop=False),
                          waits=([wv_ld] + pv_rel[vsl]) if (c == 0 and tt4 == 0) else [])
                    T(lambda e, tt4=tt4, vsl=vsl, bv_rh=bv_rh: e.matmul(
                        psb[:, 512 * vsl + 128 * tt4:512 * vsl + 128 * (tt4 + 1)],
                        ones_row[0:1, 0:128], bv_rh, start=False, stop=True),
                      inc=(tc_ if tt4 == 3 else None))
                tv = tc_.v
                if btail_pending[0] is not None:
                    fn0, info0 = btail_pending[0]
                    fn0(info0)
                btail_pending[0] = (flush_btail, (h, 0, qc, qsl, vsl, q_ready, k_ready, tv, cols))
            for wi in range(3):
                wrel[wi][sl] = (tc_, tc_.v)
                wrel[wi][sl] = (tc_, tc_.v)
        if btail_pending[0] is not None:
            fn0, info0 = btail_pending[0]
            fn0(info0)
            btail_pending[0] = None
        ar.pop()
        p.fence()

        ar.push()
        kaugs = [ar.bf16(128, S) for _ in range(2)]
        staugs = [ar.bf16(128, S) for _ in range(2)]
        att = [ar.bf16(128, HL) for _ in range(4)]
        sqc = ar.bf16(64, S)
        srow2 = ar.bf16(1, S)
        bcf = ar.f32(64, HL)
        stg = [ar.f32(65, S) for _ in range(2)]
        for z in range(2):
            V(lambda e, z=z: e.memset(kaugs[z][64:128, :], 0.0), waits=[(tc_, tc_.v)])
            V(lambda e, z=z: e.memset(kaugs[z][64:65, :], 1.0))
            V(lambda e, z=z: e.memset(staugs[z][64:128, :], 0.0))
            V(lambda e, z=z: e.memset(staugs[z][64:65, :], -1.0))

        step_vc = [(vc, 0)]
        srow_war = [(vc, 0)]

        def head_copies(hh, step):
            z = hh % 2
            ht, sub = hh // 2, hh % 2
            war = (tc_, tc_.v)
            Y(lambda e, ht=ht, sub=sub, z=z: e.dma_start(
                out=kaugs[z][0:64, :], in_=kbuf[64 * sub:64 * sub + 64, S * ht:S * (ht + 1)]),
              waits=[war], inc=kcl[z], incn=16)
            G(lambda e, ht=ht, sub=sub, z=z: e.dma_start(
                out=staugs[z][0:64, :], in_=qbuf[64 * sub:64 * sub + 64, S * ht:S * (ht + 1)]),
              waits=[war, step_vc[0]], inc=qcl[z], incn=16)
            if step == 1:
                SC(lambda e, ht=ht, sub=sub: e.activation(
                    sqc, qbuf[64 * sub:64 * sub + 64, S * ht:S * (ht + 1)], AF.Square),
                   waits=[step_vc[0], srow_war[0], (tc_, tc_.v)], inc=sc)
                for j4 in range(S // 512):
                    T(lambda e, j4=j4: e.matmul(
                        psa[0:1, 1536:2048], ones64b,
                        sqc[:, 512 * j4:512 * (j4 + 1)], start=True, stop=True),
                      waits=[(sc, sc.v)], inc=tc_)
                    SC(lambda e, j4=j4: e.activation(
                        srow2[0:1, 512 * j4:512 * (j4 + 1)],
                        psa[0:1, 1536:2048], AF.Ln),
                       waits=[(tc_, tc_.v)], inc=sc)
                SC(lambda e: e.activation(srow2, srow2, AF.Exp, scale=0.5),
                   waits=[(sc, sc.v)], inc=sc)
                V(lambda e, z=z: e.tensor_scalar(
                    staugs[z][64:65, :], srow2, -1.0, None, ALU.mult),
                  waits=[(sc, sc.v), war])
                srow_war[0] = (vc, vc.v)
            return [(kcl[z], kcl[z].v), (qcl[z], qcl[z].v), (vc, vc.v)]

        tail_rel = [(vc, 0)]
        sim_rel = [[(sc, sc.v)], [(sc, sc.v)]]
        att_rel = [[(tc_, tc_.v)] for _ in range(4)]
        acc_gate = [[(vc, vc.v)]]
        for step in range(2):
            step_vc[0] = (vc, vc.v)
            heads_ready = [None] * (NHEAD + 1)
            heads_ready[0] = head_copies(0, step)
            pending_tail = None

            def flush_tail(info):
                hh2, z2, stg_v, hin2 = info
                ht2, sub2 = hh2 // 2, hh2 % 2
                for qh2 in range(NQH):
                    qco2 = HL * qh2
                    for j2 in range(HL // 512):
                        T(lambda e, z2=z2, qco2=qco2, j2=j2: e.matmul(
                            psa[0:64, 1024:1536], onescf[64:65, :],
                            stg[z2][64:65, qco2 + 512 * j2:qco2 + 512 * (j2 + 1)],
                            start=True, stop=True),
                          waits=[(sc, stg_v[1]), (vc, vc.v)], inc=tc_)
                        V(lambda e, j2=j2: e.tensor_copy(bcf[:, 512 * j2:512 * (j2 + 1)],
                                                         psa[0:64, 1024:1536]),
                          waits=[(tc_, tc_.v)])
                    V(lambda e, z2=z2, qco2=qco2, ht2=ht2, sub2=sub2: e.tensor_tensor(
                        qbuf[64 * sub2:64 * sub2 + 64, S * ht2 + qco2:S * ht2 + qco2 + HL],
                        bcf, stg[z2][0:64, qco2:qco2 + HL], ALU.mult),
                      waits=[(vc, vc.v)] + hin2)

            for hh in range(NHEAD):
                z = hh % 2
                kaug = kaugs[z]
                stin = staugs[z]
                head_in = heads_ready[hh]
                if hh + 1 < NHEAD:
                    heads_ready[hh + 1] = head_copies(hh + 1, step)
                for qh in range(NQH):
                    qco = HL * qh

                    def emit_sim(c):
                        sb = c % 2
                        for j2 in range(HL // 512):
                            T(lambda e, c=c, sb=sb, j2=j2, qco=qco, stin=stin, kaug=kaug: e.matmul(
                                psb[:, 1024 * sb + 512 * j2:1024 * sb + 512 * (j2 + 1)],
                                kaug[0:128, 128 * c:128 * (c + 1)],
                                stin[0:128, qco + 512 * j2:qco + 512 * (j2 + 1)],
                                start=True, stop=True),
                              waits=(head_in + sim_rel[sb]) if j2 == 0 else [],
                              inc=(tc_ if j2 == HL // 512 - 1 else None))
                        return tc_.v

                    simv = [0] * n_tt
                    simv[0] = emit_sim(0)
                    if n_tt > 1:
                        simv[1] = emit_sim(1)
                    for c in range(n_tt):
                        sb = c % 2
                        ab = c % 4
                        SC(lambda e, ab=ab, sb=sb: e.activation(att[ab], psb[:, 1024 * sb:1024 * sb + HL],
                                                                AF.Exp),
                           waits=[(tc_, simv[c])] + att_rel[ab], inc=sc)
                        scv = sc.v
                        sim_rel[sb] = [(sc, scv)]
                        for j2 in range(HL // 512):
                            T(lambda e, c=c, ab=ab, j2=j2, hh=hh: e.matmul(
                                psa[0:128, 512 * j2:512 * (j2 + 1)],
                                vbuf[:, 1040 * c + 65 * hh:1040 * c + 65 * hh + 128],
                                att[ab][:, 512 * j2:512 * (j2 + 1)],
                                start=(c == 0), stop=(c == n_tt - 1)),
                              waits=([(sc, scv)] + (acc_gate[0] if c == 0 else [])) if j2 == 0 else [],
                              inc=(tc_ if (c == n_tt - 1 and j2 == HL // 512 - 1) else None))
                        att_rel[ab] = [(tc_, tc_.v)]
                        if c + 2 < n_tt:
                            simv[c + 2] = emit_sim(c + 2)
                    st_ready = tc_.v
                    V(lambda e, z=z, qco=qco: e.tensor_copy(
                        stg[z][0:64, qco:qco + HL], psa[0:64, 0:HL]),
                      waits=[(tc_, st_ready)])
                    SC(lambda e, z=z, qco=qco: e.activation(
                        stg[z][64:65, qco:qco + HL], psa[64:65, 0:HL], AF.Ln),
                       waits=[(tc_, st_ready)], inc=sc)
                    acc_gate[0] = [(vc, vc.v), (sc, sc.v)]
                    SC(lambda e, z=z, qco=qco: e.activation(
                        stg[z][64:65, qco:qco + HL], stg[z][64:65, qco:qco + HL],
                        AF.Exp, scale=-1.0), inc=sc)
                stg_v = (vc.v, sc.v)
                if pending_tail is not None:
                    flush_tail(pending_tail)
                pending_tail = (hh, z, stg_v, head_in)
            flush_tail(pending_tail)
        ar.pop()
        p.fence()

        ar.push()
        wosl = [ar.bf16(128, D) for _ in range(2)]
        xnrt = [ar.f32(128, S) for _ in range(2)]
        sstg = [ar.f32(128, 512) for _ in range(2)]
        worel = [(tc_, 0), (tc_, 0)]
        xnrel = [(vc, 0), (vc, 0)]
        srel = [(ssl_[0], 0), (ssl_[1], 0)]
        for dd in range(ND):
            sl = dd % 2
            G(lambda e, sl=sl, dd=dd: e.dma_start(out=wosl[sl], in_=wo_h[:, D * dd:D * (dd + 1)]),
              waits=[worel[sl]], inc=wol[sl], incn=16)
            wo_ld = (wol[sl], wol[sl].v)
            Y(lambda e, sl=sl, dd=dd: e.dma_start(out=xnrt[sl], in_=xnsp[dd]),
              waits=[xnrel[sl]] + a_spill_done, inc=xnl[sl], incn=16)
            xn_ld = (xnl[sl], xnl[sl].v)
            bo_lh = biasr[0:1, 3 * D + 128 * dd:3 * D + 128 * (dd + 1)]
            for qc in range(n_qc):
                ob = qc % 2
                cols = slice(512 * qc, 512 * (qc + 1))
                for c in range(ND):
                    T(lambda e, c=c, ob=ob, cols=cols, sl=sl: e.matmul(
                        bankA(ob), wosl[sl][:, 128 * c:128 * (c + 1)],
                        qbuf[:, S * c + cols.start:S * c + cols.stop],
                        start=(c == 0), stop=False),
                      waits=[wo_ld, (vc, vc.v)] if c == 0 else [])
                T(lambda e, ob=ob, bo_lh=bo_lh: e.matmul(
                    bankA(ob), bo_lh, ones_row, start=False, stop=True), inc=tc_)
                tv = tc_.v
                ssl = qc % 2
                V(lambda e, ob=ob, ssl=ssl, cols=cols, sl=sl: e.tensor_tensor(
                    sstg[ssl], bankA(ob), xnrt[sl][:, cols], ALU.add),
                  waits=[(tc_, tv), xn_ld, srel[ssl]], inc=vc)
                Y(lambda e, ssl=ssl, dd=dd, cols=cols: e.dma_start(
                    out=ssp[dd, :, cols.start:cols.stop], in_=sstg[ssl]),
                  waits=[(vc, vc.v)], inc=ssl_[ssl], incn=16)
                srel[ssl] = (ssl_[ssl], ssl_[ssl].v)
            worel[sl] = (tc_, tc_.v)
            xnrel[sl] = (vc, vc.v)
        s_spill_done = [(ssl_[0], ssl_[0].v), (ssl_[1], ssl_[1].v)]
        ar.pop()
        ar.pop()

        p.fence()
        ar.push()
        outb = ar.f32(128, ND * TH)
        for th in range(NTH):
            tho = TH * th
            ar.push()
            nxh = ar.bf16(128, ND * TH)
            wfsl = [ar.bf16(128, FSC) for _ in range(2)]
            wsl1 = [t[:, 0:8 * FC] for t in wfsl]
            wsl3 = [t[:, 8 * FC:16 * FC] for t in wfsl]
            wsl2 = [t[:, 16 * FC:16 * FC + KC * D] for t in wfsl]
            b1sl = [t[:, 16 * FC + KC * D:17 * FC + KC * D] for t in wfsl]
            b3sl = [t[:, 17 * FC + KC * D:18 * FC + KC * D] for t in wfsl]
            a2sl = [t[:, 18 * FC + KC * D:18 * FC + KC * D + KC * 128] for t in wfsl]
            trsel = [[ar.bf16(128, TH) for _ in range(2)] for _ in range(2)]
            wexp = [ar.f32(128, TH) for _ in range(2)]
            mex = [ar.f32(128, TH) for _ in range(2)]
            rrs = [ar.f32(128, TH) for _ in range(2)]
            rrb = [ar.bf16(128, TH) for _ in range(2)]
            act2w = [ar.bf16(128, KC * TC) for _ in range(2)]
            silu_s = ar.f32(128, KC * TC)
            h3w_s = ar.f32(128, KC * TC)
            glog = ar.f32(128, 8)
            m1c = ar.f32(128, 1)
            m2c = ar.f32(128, 1)
            e2c = ar.f32(128, 1)
            w1c = ar.f32(128, 1)
            w2c_ = ar.f32(128, 1)
            msk = [ar.f32(128, 8) for _ in range(2)]
            tmp8 = ar.f32(128, 8)
            mrow = [ar.f32(8, TH) for _ in range(2)]
            wrow = [ar.f32(1, TH) for _ in range(2)]

            for c in range(ND):
                Y(lambda e, c=c, tho=tho: e.dma_start(
                    out=nxh[:, TH * c:TH * (c + 1)], in_=nxsp[c, :, tho:tho + TH]),
                  waits=a_spill_done, inc=nxl, incn=16)
            nx_ld = (nxl, nxl.v)

            for jt in range(TH // 128):
                gb = jt % 2
                for c in range(ND):
                    T(lambda e, c=c, jt=jt, gb=gb: e.matmul(
                        psa[:, 512 * gb:512 * gb + 8],
                        nxh[:, TH * c + 128 * jt:TH * c + 128 * (jt + 1)],
                        gw[:, 8 * c:8 * (c + 1)],
                        start=(c == 0), stop=(c == ND - 1)),
                      waits=[nx_ld, (vc, vc.v), consts_lg] if c == 0 else [],
                      inc=(tc_ if c == ND - 1 else None))
                gv = tc_.v
                V(lambda e, gb=gb: e.tensor_copy(glog, psa[:, 512 * gb:512 * gb + 8]),
                  waits=[(tc_, gv), (vc, vc.v)], inc=vc)
                V(lambda e: e.reduce_max(m1c, glog, axis=AX.X), waits=[(vc, vc.v)], inc=vc)
                V(lambda e: e.tensor_scalar(msk[0], glog, m1c, None, ALU.is_equal),
                  waits=[(vc, vc.v)], inc=vc)
                V(lambda e: e.tensor_scalar(tmp8, msk[0], -1e30, None, ALU.mult),
                  waits=[(vc, vc.v)], inc=vc)
                V(lambda e: e.tensor_tensor(tmp8, glog, tmp8, ALU.add), waits=[(vc, vc.v)], inc=vc)
                V(lambda e: e.reduce_max(m2c, tmp8, axis=AX.X), waits=[(vc, vc.v)], inc=vc)
                V(lambda e: e.tensor_scalar(msk[1], glog, m2c, None, ALU.is_equal),
                  waits=[(vc, vc.v)], inc=vc)
                V(lambda e: e.tensor_tensor(e2c, m2c, m1c, ALU.subtract), waits=[(vc, vc.v)], inc=vc)
                SC(lambda e: e.activation(e2c, e2c, AF.Exp), waits=[(vc, vc.v)], inc=sc)
                V(lambda e: e.tensor_scalar(w1c, e2c, 1.0, None, ALU.add), waits=[(sc, sc.v)], inc=vc)
                V(lambda e: e.reciprocal(w1c, w1c), waits=[(vc, vc.v)], inc=vc)
                V(lambda e: e.tensor_tensor(w2c_, e2c, w1c, ALU.mult), waits=[(vc, vc.v)], inc=vc)
                mv = vc.v
                for j in range(2):
                    T(lambda e, j=j: e.transpose(
                        psa[0:8, 1024 + 128 * j:1024 + 128 * (j + 1)], msk[j], ident),
                      waits=[(vc, mv), (vc, vc.v)], inc=tc_)
                    V(lambda e, j=j, jt=jt: e.tensor_copy(
                        mrow[j][:, 128 * jt:128 * (jt + 1)],
                        psa[0:8, 1024 + 128 * j:1024 + 128 * (j + 1)]),
                      waits=[(tc_, tc_.v)], inc=vc)
                for j, wcol in enumerate((w1c, w2c_)):
                    T(lambda e, j=j, wcol=wcol: e.transpose(
                        psa[0:1, 1536 + 128 * j:1536 + 128 * (j + 1)], wcol, ident),
                      waits=[(vc, mv), (vc, vc.v)], inc=tc_)
                    V(lambda e, j=j, jt=jt: e.tensor_copy(
                        wrow[j][:, 128 * jt:128 * (jt + 1)],
                        psa[0:1, 1536 + 128 * j:1536 + 128 * (j + 1)]),
                      waits=[(tc_, tc_.v)], inc=vc)
            rows_done = (vc, vc.v)
            for j in range(2):
                for ch in range(TH // 512):
                    T(lambda e, j=j, ch=ch: e.matmul(
                        bankB(0), ones1f, wrow[j][0:1, 512 * ch:512 * (ch + 1)],
                        start=True, stop=True),
                      waits=[rows_done, (vc, vc.v)], inc=tc_)
                    V(lambda e, j=j, ch=ch: e.tensor_copy(
                        wexp[j][:, 512 * ch:512 * (ch + 1)], bankB(0)),
                      waits=[(tc_, tc_.v)], inc=vc)
                    T(lambda e, j=j, ch=ch: e.matmul(
                        bankB(1), e8, mrow[j][0:8, 512 * ch:512 * (ch + 1)],
                        start=True, stop=True),
                      waits=[(vc, vc.v)], inc=tc_)
                    V(lambda e, j=j, ch=ch: e.tensor_copy(
                        mex[j][:, 512 * ch:512 * (ch + 1)], bankB(1)),
                      waits=[(tc_, tc_.v)], inc=vc)
            bc_done = (vc, vc.v)

            for br, a_sb in enumerate((a1t, a3t)):
                for ch in range(TH // 512):
                    tb = ch % 2
                    for c in range(ND):
                        T(lambda e, c=c, ch=ch, tb=tb, a_sb=a_sb: e.matmul(
                            bankA(tb), a_sb[:, 128 * c:128 * (c + 1)],
                            nxh[:, TH * c + 512 * ch:TH * c + 512 * (ch + 1)],
                            start=(c == 0), stop=(c == ND - 1)),
                          waits=[(vc, vc.v), consts_lg] if c == 0 else [],
                          inc=(tc_ if c == ND - 1 else None))
                    trv = tc_.v
                    for j in range(2):
                        V(lambda e, br=br, j=j, ch=ch, tb=tb: e.tensor_tensor(
                            trsel[br][j][:, 512 * ch:512 * (ch + 1)], bankA(tb),
                            mex[j][:, 512 * ch:512 * (ch + 1)], ALU.mult),
                          waits=[(tc_, trv), bc_done], inc=vc)
            trsel_done = (vc, vc.v)
            for j in range(2):
                V(lambda e, j=j: e.memset(rrs[j][:, :], 0.0), waits=[(tc_, tc_.v)], inc=vc)
            rr_zero = (vc, vc.v)

            w_rel = [(tc_, 0), (tc_, 0)]
            grel = [[], []]
            for fc in range(NFC):
                sl = fc % 2
                G(lambda e, sl=sl, fc=fc: e.dma_start(
                    out=wfsl[sl], in_=wfs_h[:, FSC * fc:FSC * (fc + 1)]),
                  waits=[w_rel[sl]], inc=w1l[sl], incn=16)
                sl_ld = [(w1l[sl], w1l[sl].v)]
                for tcc in range(n_tc):
                    tco = TC * tcc
                    G0, G1 = bankA, bankB
                    silu_v = [0, 0]
                    a2v = [0, 0]

                    def emit_h(j, grp, gwaits):
                        h1v = 0
                        for br in range(2):
                            wslb = (wsl1 if br == 0 else wsl3)[sl]
                            bslb = (b1sl if br == 0 else b3sl)[sl]
                            for m in range(KC):
                                bank_i = KC * br + m
                                for c in range(ND):
                                    T(lambda e, c=c, m=m, grp=grp, bank_i=bank_i,
                                        wslb=wslb, tco=tco: e.matmul(
                                        grp(bank_i),
                                        wslb[:, FC * c + 128 * m:FC * c + 128 * (m + 1)],
                                        nxh[:, TH * c + tco:TH * c + tco + TC],
                                        start=(c == 0), stop=False),
                                      waits=gwaits if (c == 0 and m == 0 and br == 0) else [])
                                T(lambda e, m=m, br=br, j=j, grp=grp, bank_i=bank_i,
                                    bslb=bslb, tco=tco: e.matmul(
                                    grp(bank_i),
                                    bslb[:, 128 * m:128 * (m + 1)],
                                    trsel[br][j][:, tco:tco + TC],
                                    start=False, stop=True),
                                  inc=(tc_ if m == KC - 1 else None))
                                if br == 0 and m == KC - 1:
                                    h1v = tc_.v
                        return h1v, tc_.v

                    def emit_evac(j, grp, h1v, hv):
                        for bi in range(KC):
                            SC(lambda e, bi=bi, grp=grp: e.activation(
                                silu_s[:, 512 * bi:512 * (bi + 1)], grp(bi), AF.Silu),
                               waits=[(tc_, h1v), (vc, vc.v)], inc=sc)
                        silu_v[j] = sc.v
                        for bi in range(KC):
                            V(lambda e, bi=bi, grp=grp, j=j, tco=tco: e.tensor_tensor(
                                h3w_s[:, 512 * bi:512 * (bi + 1)],
                                grp(KC + bi),
                                wexp[j][:, tco:tco + TC], ALU.mult),
                              waits=[(tc_, hv)])
                        V(lambda e, j=j: e.tensor_tensor(act2w[j], silu_s, h3w_s, ALU.mult),
                          waits=[(sc, silu_v[j])])
                        a2v[j] = vc.v

                    def emit_rr(j, rbank):
                        for c4 in range(KC):
                            T(lambda e, c4=c4, j=j, rbank=rbank, sl=sl: e.matmul(
                                rbank, a2sl[sl][:, 128 * c4:128 * (c4 + 1)],
                                act2w[j][:, TC * c4:TC * (c4 + 1)],
                                start=(c4 == 0), stop=(c4 == KC - 1)),
                              waits=([(vc, a2v[j]), (sc, silu_v[1]), (vc, vc.v)] + sl_ld)
                              if c4 == 0 else [],
                              inc=(tc_ if c4 == KC - 1 else None))
                        rrv = tc_.v
                        V(lambda e, j=j, rrv=rrv, rbank=rbank, tco=tco: e.tensor_tensor(
                            rrs[j][:, tco:tco + TC], rbank, rrs[j][:, tco:tco + TC], ALU.add),
                          waits=[(tc_, rrv), rr_zero])

                    def emit_w2_half(grp, gten, ddh):
                        for dd4 in range(4):
                            dd = 4 * ddh + dd4
                            for j in range(2):
                                for c4 in range(KC):
                                    T(lambda e, dd=dd, dd4=dd4, c4=c4, j=j, grp=grp,
                                        sl=sl: e.matmul(
                                        grp(dd4),
                                        wsl2[sl][:, D * c4 + 128 * dd:D * c4 + 128 * (dd + 1)],
                                        act2w[j][:, TC * c4:TC * (c4 + 1)],
                                        start=(j == 0 and c4 == 0), stop=(j == 1 and c4 == KC - 1)),
                                      waits=([(vc, a2v[1]), (sc, silu_v[1]), (vc, a2v[0]),
                                              (vc, vc.v)] + sl_ld)
                                      if (c4 == 0 and dd4 == 0 and j == 0) else [],
                                      inc=(tc_ if (j == 1 and c4 == KC - 1) else None))
                            wv = tc_.v
                            if fc == 0:
                                V(lambda e, dd=dd, dd4=dd4, grp=grp, tco=tco, wv=wv: e.tensor_copy(
                                    outb[:, TH * dd + tco:TH * dd + tco + TC], grp(dd4)),
                                  waits=[(tc_, wv)])
                            else:
                                V(lambda e, dd=dd, dd4=dd4, grp=grp, tco=tco, wv=wv: e.tensor_tensor(
                                    outb[:, TH * dd + tco:TH * dd + tco + TC], grp(dd4),
                                    outb[:, TH * dd + tco:TH * dd + tco + TC], ALU.add),
                                  waits=[(tc_, wv)])

                    h1v0, hv0 = emit_h(0, G0, sl_ld + [trsel_done] + grel[0])
                    emit_evac(0, G0, h1v0, hv0)
                    h1v1, hv1 = emit_h(1, G1, grel[1])
                    emit_evac(1, G1, h1v1, hv1)
                    emit_rr(0, G1(0, TC))
                    emit_w2_half(G0, psa, 0)
                    emit_rr(1, G1(1, TC))
                    emit_w2_half(G1, psb, 1)
                    grel[0] = [(vc, vc.v)]
                    grel[1] = [(vc, vc.v)]
                w_rel[sl] = (tc_, tc_.v)
            for j in range(2):
                V(lambda e, j=j: e.tensor_tensor(rrs[j], rrs[j], mex[j], ALU.mult),
                  waits=[(vc, vc.v)], inc=vc)
                V(lambda e, j=j: e.tensor_copy(rrb[j], rrs[j]), waits=[(vc, vc.v)], inc=vc)
            rbv = vc.v
            for tcc in range(TH // 512):
                tco = 512 * tcc
                for dd in range(ND):
                    bank_i = dd % 4
                    for j in range(2):
                        T(lambda e, j=j, dd=dd, tco=tco, bank_i=bank_i: e.matmul(
                            bankA(bank_i), b2c[:, 128 * dd:128 * (dd + 1)],
                            rrb[j][:, tco:tco + 512],
                            start=(j == 0), stop=(j == 1)),
                          waits=[(vc, rbv), (vc, vc.v), consts_lg] if j == 0 else [],
                          inc=(tc_ if j == 1 else None))
                    V(lambda e, dd=dd, tco=tco, bank_i=bank_i: e.tensor_tensor(
                        outb[:, TH * dd + tco:TH * dd + tco + 512],
                        bankA(bank_i),
                        outb[:, TH * dd + tco:TH * dd + tco + 512], ALU.add),
                      waits=[(tc_, tc_.v)], inc=vc)
            ar.pop()
            p.fence()
            ffn_done = (vc, vc.v)

            ar.push()
            sgh = ar.f32(128, ND * TH)
            stile = [ar.f32(128, TH) for _ in range(2)]
            gtile = ar.f32(128, TH)
            ofin = [ar.f32(128, D) for _ in range(2)]
            xrt = [ar.f32(128, D) for _ in range(2)]
            strel = [(vc, 0), (vc, 0)]
            for dd in range(ND):
                sl = dd % 2
                Y(lambda e, dd=dd, sl=sl, tho=tho: e.dma_start(
                    out=stile[sl], in_=ssp[dd, :, tho:tho + TH]),
                  waits=[strel[sl]] + s_spill_done, inc=stl[sl], incn=16)
                st_ld = (stl[sl], stl[sl].v)
                SC(lambda e, dd=dd: e.activation(gtile, outb[:, TH * dd:TH * (dd + 1)], AF.Silu),
                   waits=[ffn_done, (vc, vc.v)], inc=sc)
                V(lambda e, dd=dd, sl=sl: e.tensor_tensor(
                    sgh[:, TH * dd:TH * (dd + 1)], stile[sl], gtile, ALU.mult),
                  waits=[(sc, sc.v), st_ld], inc=vc)
                strel[sl] = (vc, vc.v)
            sg_done = (vc, vc.v)
            orel_f = [(ofs[0], 0), (ofs[1], 0)]
            for jt in range(TH // 128):
                osl = jt % 2
                Y(lambda e, jt=jt, osl=osl, tho=tho, b=b: e.dma_start(
                    out=xrt[osl], in_=x_h[b, tho + 128 * jt:tho + 128 * (jt + 1), :]),
                  waits=[orel_f[osl]], inc=xfl[osl], incn=16)
                xf_ld = (xfl[osl], xfl[osl].v)
                for dd in range(ND):
                    tb = dd % 2
                    T(lambda e, jt=jt, dd=dd, tb=tb: e.transpose(
                        bankB(tb, 128), sgh[:, TH * dd + 128 * jt:TH * dd + 128 * (jt + 1)], ident),
                      waits=[sg_done, (vc, vc.v)], inc=tc_)
                    V(lambda e, jt=jt, dd=dd, osl=osl, tb=tb: e.tensor_tensor(
                        ofin[osl][:, 128 * dd:128 * (dd + 1)], bankB(tb, 128),
                        xrt[osl][:, 128 * dd:128 * (dd + 1)], ALU.add),
                      waits=[(tc_, tc_.v), xf_ld], inc=vc)
                Y(lambda e, jt=jt, osl=osl, tho=tho, b=b: e.dma_start(
                    out=out_h[b, tho + 128 * jt:tho + 128 * (jt + 1), :], in_=ofin[osl]),
                  waits=[(vc, vc.v)], inc=ofs[osl], incn=16)
                orel_f[osl] = (ofs[osl], ofs[osl].v)
            ar.pop()
            p.fence()
        ar.pop()

    final_sems = [(ofs[0], ofs[0].v), (ofs[1], ofs[1].v),
                  (ssl_[0], ssl_[0].v), (ssl_[1], ssl_[1].v)] +                  [(stsp[c], stsp[c].v) for c in range(ND)]
    p.op("sync", lambda e: None, waits=final_sems)
    p.finalize()
    return nc


def prep_weights(i):
    def bf(x):
        return np.ascontiguousarray(x, dtype=np.float32)

    w = {}
    for nm, W in (("wq", i["L0_attn_Wq"]), ("wk", i["L0_attn_Wk"]),
                  ("wv", i["L0_attn_Wv"]), ("wo", i["L0_attn_Wo"])):
        w[nm] = bf(np.asarray(W).reshape(8, 128, 8, 128).transpose(1, 2, 0, 3).reshape(128, 8192))
    w["bias"] = bf(np.concatenate(
        [np.asarray(i["L0_attn_b" + k]).reshape(1, D) for k in "qkv"]
        + [np.asarray(i["L0_attn_bo"]).reshape(1, D)], axis=1))
    lnw = np.asarray(i["L0_attn_ln_w"]).reshape(8, 128).T
    lnb = np.asarray(i["L0_attn_ln_b"]).reshape(8, 128).T
    nwc = np.asarray(i["L0_ffn_norm_w"]).reshape(8, 128).T
    w["lncol"] = bf(np.concatenate([lnw, lnb, nwc], axis=1))
    w1t = np.asarray(i["L0_ffn_w1"]).T
    w3t = np.asarray(i["L0_ffn_w3"]).T
    w["w1t"] = bf(w1t.reshape(8, 128, NFC, FC).transpose(1, 2, 0, 3).reshape(128, NFC * 8 * FC))
    w["w3t"] = bf(w3t.reshape(8, 128, NFC, FC).transpose(1, 2, 0, 3).reshape(128, NFC * 8 * FC))
    w2t = np.asarray(i["L0_ffn_w2"]).T
    w["w2t"] = bf(w2t.reshape(NFC, KC, 128, D).transpose(2, 0, 1, 3).reshape(128, NFC * KC * D))
    w["b1c"] = bf(np.asarray(i["L0_ffn_b1"]).transpose(0, 2, 1).reshape(ER, FF))
    w["b3c"] = bf(np.asarray(i["L0_ffn_b3"]).transpose(0, 2, 1).reshape(ER, FF))
    a2t = np.asarray(i["L0_ffn_a2"]).reshape(ER, FF).T
    w["a2t"] = bf(a2t.reshape(NFC, KC, 128, ER).transpose(2, 0, 1, 3).reshape(128, NFC * KC * ER))
    a1t = np.asarray(i["L0_ffn_a1"]).reshape(ER, D).T
    a3t = np.asarray(i["L0_ffn_a3"]).reshape(ER, D).T
    w["a1t"] = bf(a1t.reshape(8, 128, 128).transpose(1, 0, 2).reshape(128, 8 * 128))
    w["a3t"] = bf(a3t.reshape(8, 128, 128).transpose(1, 0, 2).reshape(128, 8 * 128))
    w["b2c"] = bf(np.asarray(i["L0_ffn_b2"]).transpose(0, 2, 1).reshape(ER, D))
    FSC = 8 * FC + 8 * FC + KC * D + FC + FC + KC * 128
    fs = np.empty((128, NFC * FSC), np.float32)
    for fc in range(NFC):
        o = fc * FSC
        fs[:, o:o + 8 * FC] = w["w1t"][:, 8 * FC * fc:8 * FC * (fc + 1)]
        o += 8 * FC
        fs[:, o:o + 8 * FC] = w["w3t"][:, 8 * FC * fc:8 * FC * (fc + 1)]
        o += 8 * FC
        fs[:, o:o + KC * D] = w["w2t"][:, KC * D * fc:KC * D * (fc + 1)]
        o += KC * D
        fs[:, o:o + FC] = w["b1c"][:, FC * fc:FC * (fc + 1)]
        o += FC
        fs[:, o:o + FC] = w["b3c"][:, FC * fc:FC * (fc + 1)]
        o += FC
        fs[:, o:o + KC * 128] = w["a2t"][:, KC * 128 * fc:KC * 128 * (fc + 1)]
    w["wfs"] = np.ascontiguousarray(fs)
    gwt = np.asarray(i["L0_ffn_gate_w"]).T
    w["gw"] = bf(gwt.reshape(8, 128, 8).transpose(1, 0, 2).reshape(128, 64))
    e2m = np.zeros((2, 128), np.float32)
    e2m[0, 0:64] = 1.0
    e2m[1, 64:128] = 1.0
    w["e2"] = e2m
    e8m = np.zeros((8, 128), np.float32)
    for ee in range(8):
        e8m[ee, 16 * ee:16 * (ee + 1)] = 2.0
    w["e8"] = e8m
    w["ident"] = np.eye(128, dtype=np.float32)
    return w


_CACHE = {}


def kernel(**inputs):
    S, NB, NCORE = 2048, 2, 8
    key = (S, NB)
    if key not in _CACHE:
        _CACHE[key] = build_program(S, NB)
    nc = _CACHE[key]
    w = prep_weights(inputs)
    x = np.ascontiguousarray(np.asarray(inputs["L0_attn_hidden_states"], dtype=np.float32))
    in_maps = []
    for cid in range(NCORE):
        m = dict(w)
        m["x"] = np.ascontiguousarray(x[NB * cid:NB * (cid + 1)])
        in_maps.append(m)
    res = run_bass_kernel_spmd(nc, in_maps, core_ids=list(range(NCORE)))
    out = np.concatenate([np.asarray(r["out"], dtype=np.float32) for r in res.results], axis=0)
    return out


# version 20 (final)
# speedup vs baseline: 1.2073x; 1.1850x over previous
import numpy as np
from collections import defaultdict

import concourse.bass as bass
import concourse.mybir as mybir
from concourse.bass_utils import run_bass_kernel_spmd

F32 = mybir.dt.float32
BF16 = mybir.dt.bfloat16
U8 = mybir.dt.uint8
AF = mybir.ActivationFunctionType
ALU = mybir.AluOpType
AX = mybir.AxisListType

D = 1024
ND = 8
FF = 4096
HD = 64
NHEAD = 16
NPAIR = 8
ER = 128
FC = 256
KC = FC // 128
NFC = FF // FC
TC = 512


class Ctr:
    __slots__ = ("sem", "v", "name")

    def __init__(self, nc, name):
        self.sem = nc.alloc_semaphore(name)
        self.v = 0
        self.name = name


class Prog:
    ENGINES = ("sync", "gpsimd", "tensor", "vector", "scalar")

    def __init__(self, nc, dedup=True):
        self.nc = nc
        self.dedup = dedup
        self.streams = defaultdict(list)
        self._waited = defaultdict(dict)
        self.fctr = {}
        self.dma_ctrs_sync = []
        self.dma_ctrs_gp = []
        self.own_ctr = {}

    def ctr(self, name):
        return Ctr(self.nc, name)

    def op(self, eng, fn, waits=(), inc=None, incn=1, noinc=False):
        own = self.own_ctr.get(eng)
        wlist = []
        if own is not None and own.v > 0:
            wlist.append((own, own.v))
        for w in waits:
            c, v = w if isinstance(w, tuple) else (w, w.v)
            if v > 0:
                wlist.append((c, v))
        if own is not None and not noinc:
            assert inc is None or inc is own
            inc = own
        if inc is not None:
            inc.v += incn

        def run(engine, _w=wlist, _inc=inc, _incn=incn, _fn=fn, _eng=eng):
            seen = self._waited[_eng]
            for c, v in _w:
                if self.dedup and seen.get(c.name, 0) >= v:
                    continue
                engine.wait_ge(c.sem, v)
                seen[c.name] = v
            inst = _fn(engine)
            if _inc is not None:
                inst.then_inc(_inc.sem, _incn)
            return inst

        self.streams[eng].append(run)

    def fence(self, engines=None):
        engines = engines or self.ENGINES
        for e in engines:
            if e not in self.fctr:
                self.fctr[e] = self.ctr("f_" + e)
        for e, ctrs in (("sync", self.dma_ctrs_sync), ("gpsimd", self.dma_ctrs_gp)):
            if e in engines:
                self.op(e, lambda eng: None, waits=[(c, c.v) for c in ctrs], noinc=True)
        for e in engines:
            if e in self.own_ctr:
                self.op(e, lambda eng: eng.nop())
            else:
                self.op(e, lambda eng: eng.nop(), inc=self.fctr[e])
        snaps = []
        for e in engines:
            c = self.own_ctr.get(e) or self.fctr[e]
            snaps.append((e, c, c.v))
        for e in engines:
            ws = [(c, v) for (e2, c, v) in snaps if e2 != e]
            self.op(e, lambda eng: None, waits=ws, noinc=True)

    def finalize(self):
        with self.nc.Block() as block:
            for eng in self.ENGINES:
                fns = self.streams.get(eng)
                if not fns:
                    continue

                def make(fns=fns):
                    def body(engine):
                        for f in fns:
                            f(engine)
                    return body

                getattr(block, eng)(make())


class Arena:
    def __init__(self, ap, nbytes):
        self.base = ap
        self.n = nbytes
        self.off = 0
        self.stack = []
        self.peak = 0

    def push(self):
        self.stack.append(self.off)

    def pop(self):
        self.off = self.stack.pop()

    def _al(self, parts, cols, esz, dt):
        self.off = (self.off + 31) & ~31
        nb = cols * esz
        v = self.base[:, self.off:self.off + nb].bitcast(dt)
        self.off += nb
        self.peak = max(self.peak, self.off)
        assert self.off <= self.n, f"arena overflow {self.off} > {self.n}"
        if parts != 128:
            v = v[0:parts, :]
        return v

    def f32(self, parts, cols):
        return self._al(parts, cols, 4, F32)

    def bf16(self, parts, cols):
        return self._al(parts, cols, 2, BF16)


def build_program(S=2048, NB=2, dedup=True):
    n_tt = S // 128
    HL = min(S, 1024)
    NQH = S // HL
    TH = min(S, 1024)
    NTH = S // TH
    n_tc = TH // TC
    n_qc = S // 512

    nc = bass.Bass("TRN2", target_bir_lowering=False)

    x_h = nc.dram_tensor("x", [NB, S, D], F32, kind="ExternalInput").ap()
    out_h = nc.dram_tensor("out", [NB, S, D], F32, kind="ExternalOutput").ap()

    def win(name, shape):
        return nc.dram_tensor(name, shape, F32, kind="ExternalInput").ap()

    wq_h = win("wq", [128, 8 * D])
    wk_h = win("wk", [128, 8 * D])
    wv_h = win("wv", [128, 8 * D])
    wo_h = win("wo", [128, 8 * D])
    bias_h = win("bias", [1, 4 * D])
    lncol_h = win("lncol", [128, 24])
    w1t_h = win("w1t", [128, NFC * 8 * FC])
    w3t_h = win("w3t", [128, NFC * 8 * FC])
    w2t_h = win("w2t", [128, NFC * KC * D])
    b1c_h = win("b1c", [128, FF])
    b3c_h = win("b3c", [128, FF])
    a2t_h = win("a2t", [128, NFC * KC * 128])
    FSC = 8 * FC + 8 * FC + KC * D + FC + FC + KC * 128
    wfs_h = win("wfs", [128, NFC * FSC])
    a1t_h = win("a1t", [128, ND * 128])
    a3t_h = win("a3t", [128, ND * 128])
    b2c_h = win("b2c", [128, ND * 128])
    gw_h = win("gw", [128, ND * 8])
    e2_h = win("e2", [2, 128])
    e8_h = win("e8", [8, 128])
    id_h = win("ident", [128, 128])

    xnsp = nc.dram_tensor("xnsp", [ND, 128, S], F32, kind="Internal").ap()
    nxsp = nc.dram_tensor("nxsp", [ND, 128, S], BF16, kind="Internal").ap()
    ssp = nc.dram_tensor("ssp", [ND, 128, S], F32, kind="Internal").ap()

    ones_row = nc.alloc_sbuf_tensor("ones_row", [1, 512], BF16).ap()
    ones1f = nc.alloc_sbuf_tensor("ones1f", [1, 128], F32).ap()
    onescf = nc.alloc_sbuf_tensor("onescf", [128, 64], F32).ap()
    ones64b = nc.alloc_sbuf_tensor("ones64b", [64, 1], BF16).ap()
    ind2 = nc.alloc_sbuf_tensor("ind2", [128, 2], BF16).ap()
    e2 = nc.alloc_sbuf_tensor("e2s", [2, 128], F32).ap()
    e8 = nc.alloc_sbuf_tensor("e8s", [8, 128], F32).ap()
    ident = nc.alloc_sbuf_tensor("idents", [128, 128], F32).ap()
    biasr = nc.alloc_sbuf_tensor("biasrs", [1, 4 * D], BF16).ap()
    lncol = nc.alloc_sbuf_tensor("lncols", [128, 24], F32).ap()
    gw = nc.alloc_sbuf_tensor("gws", [128, ND * 8], BF16).ap()
    a1t = nc.alloc_sbuf_tensor("a1ts", [128, ND * 128], BF16).ap()
    a3t = nc.alloc_sbuf_tensor("a3ts", [128, ND * 128], BF16).ap()
    b2c = nc.alloc_sbuf_tensor("b2cs", [128, ND * 128], BF16).ap()
    statc = nc.alloc_sbuf_tensor("statc", [128, 16 * 8], F32).ap()

    psa = nc.alloc_psum_tensor("psa", [128, 2048], F32).ap()
    psb = nc.alloc_psum_tensor("psb", [128, 2048], F32).ap()

    def bankA(i, cols=512):
        return psa[:, 512 * i:512 * i + cols]

    def bankB(i, cols=512):
        return psb[:, 512 * i:512 * i + cols]

    rem = nc._state.sbuf_bytes_remaining - 1024
    arena_t = nc.alloc_sbuf_tensor("arena", [128, rem], U8).ap()
    ar = Arena(arena_t, rem)

    p = Prog(nc, dedup=dedup)
    tc_ = p.ctr("tc")
    vc = p.ctr("vc")
    sc = p.ctr("sc")
    c_ld = p.ctr("c_ld")
    c_lg = p.ctr("c_lg")
    xld = [p.ctr("xld0"), p.ctr("xld1")]
    stsp = [p.ctr(f"stsp{c}") for c in range(ND)]
    wql = [p.ctr("wql0"), p.ctr("wql1")]
    wkl = [p.ctr("wkl0"), p.ctr("wkl1")]
    wvl = [p.ctr("wvl0"), p.ctr("wvl1")]
    wol = [p.ctr("wol0"), p.ctr("wol1")]
    xnl = [p.ctr("xnl0"), p.ctr("xnl1")]
    ssl_ = [p.ctr("ssl0"), p.ctr("ssl1")]
    nxl = p.ctr("nxl")
    w1l = [p.ctr("w1l0"), p.ctr("w1l1")]
    w3l = [p.ctr("w3l0"), p.ctr("w3l1")]
    w2l = [p.ctr("w2l0"), p.ctr("w2l1")]
    bl1 = [p.ctr("bl10"), p.ctr("bl11")]
    bl3 = [p.ctr("bl30"), p.ctr("bl31")]
    a2l = [p.ctr("a2l0"), p.ctr("a2l1")]
    stl = [p.ctr("stl0"), p.ctr("stl1")]
    xfl = [p.ctr("xfl0"), p.ctr("xfl1")]
    ofs = [p.ctr("ofs0"), p.ctr("ofs1")]
    kcl = [p.ctr("kcl0"), p.ctr("kcl1")]
    qcl = [p.ctr("qcl0"), p.ctr("qcl1")]
    p.own_ctr = {"vector": vc, "scalar": sc}
    p.dma_ctrs_sync = [c_ld] + xld + stsp + xnl + ssl_ + [nxl] + stl + xfl + ofs + kcl
    p.dma_ctrs_gp = [c_lg] + wql + wkl + wvl + wol + w1l + w3l + w2l + bl1 + bl3 + a2l + qcl

    def T(fn, waits=(), inc=None, incn=1):
        p.op("tensor", fn, waits, inc, incn)

    def V(fn, waits=(), inc=None, incn=1):
        p.op("vector", fn, waits, inc, incn)

    def SC(fn, waits=(), inc=None, incn=1):
        p.op("scalar", fn, waits, inc, incn)

    def Y(fn, waits=(), inc=None, incn=1):
        p.op("sync", fn, waits, inc, incn)

    def G(fn, waits=(), inc=None, incn=1):
        p.op("gpsimd", fn, waits, inc, incn)

    V(lambda e: e.memset(ones_row[:, :], 1.0), inc=vc)
    V(lambda e: e.memset(ones64b[:, :], 1.0), inc=vc)
    V(lambda e: e.memset(ones1f[:, :], 1.0), inc=vc)
    V(lambda e: e.memset(onescf[:, :], 1.0), inc=vc)
    V(lambda e: e.memset(ind2[:, :], 0.0), inc=vc)
    V(lambda e: e.memset(ind2[0:64, 0:1], 1.0), waits=[vc], inc=vc)
    V(lambda e: e.memset(ind2[64:128, 1:2], 1.0), waits=[vc], inc=vc)
    Y(lambda e: e.dma_start(out=e2, in_=e2_h), inc=c_ld, incn=16)
    Y(lambda e: e.dma_start(out=e8, in_=e8_h), inc=c_ld, incn=16)
    Y(lambda e: e.dma_start(out=ident, in_=id_h), inc=c_ld, incn=16)
    Y(lambda e: e.dma_start(out=lncol, in_=lncol_h), inc=c_ld, incn=16)
    G(lambda e: e.dma_start(out=biasr, in_=bias_h), inc=c_lg, incn=16)
    G(lambda e: e.dma_start(out=gw, in_=gw_h), inc=c_lg, incn=16)
    G(lambda e: e.dma_start(out=a1t, in_=a1t_h), inc=c_lg, incn=16)
    G(lambda e: e.dma_start(out=a3t, in_=a3t_h), inc=c_lg, incn=16)
    G(lambda e: e.dma_start(out=b2c, in_=b2c_h), inc=c_lg, incn=16)
    consts_v = vc.v
    consts_ld = (c_ld, c_ld.v)
    consts_lg = (c_lg, c_lg.v)

    for b in range(NB):
        ar.push()
        x2T = ar.bf16(128, ND * S)
        qbuf = ar.bf16(128, ND * S)
        kbuf = ar.bf16(128, ND * S)
        vbuf = ar.bf16(128, n_tt * NHEAD * 65 + 63)

        ar.push()
        xt = [ar.f32(128, D) for _ in range(2)]
        sqf = ar.f32(128, D)
        xnb = ar.f32(128, D)
        x2p = ar.f32(128, D)
        nxp = ar.f32(128, D)
        xnstg = [ar.f32(128, 512) for _ in range(ND)]
        nxstg = [ar.bf16(128, 512) for _ in range(ND)]

        xrel = [(vc, 0), (vc, 0)]
        for i in range(n_tt):
            sl = i % 2
            Y(lambda e, sl=sl, i=i, b=b: e.dma_start(out=xt[sl], in_=x_h[b, 128 * i:128 * (i + 1), :]),
              waits=[xrel[sl]], inc=xld[sl], incn=16)
            ldw = (xld[sl], xld[sl].v)
            SC(lambda e, sl=sl: e.activation(sqf, xt[sl], AF.Square),
               waits=[ldw, (vc, vc.v)], inc=sc)
            V(lambda e, sl=sl, i=i: e.reduce_sum(statc[:, i:i + 1], xt[sl], axis=AX.X),
              waits=[ldw], inc=vc)
            V(lambda e, i=i: e.reduce_sum(statc[:, 16 + i:17 + i], sqf, axis=AX.X),
              waits=[(sc, sc.v)], inc=vc)
            xrel[sl] = (vc, vc.v)

        cs = statc

        def col(k):
            return cs[:, 16 * k:16 * k + n_tt]

        V(lambda e: e.tensor_scalar(col(2), col(0), 1.0 / D, None, ALU.mult), waits=[vc], inc=vc)
        V(lambda e: e.tensor_tensor(col(7), col(2), col(2), ALU.mult), waits=[vc], inc=vc)
        V(lambda e: e.tensor_scalar(col(3), col(1), 1.0 / D, None, ALU.mult), waits=[vc], inc=vc)
        V(lambda e: e.tensor_tensor(col(3), col(3), col(7), ALU.subtract), waits=[vc], inc=vc)
        V(lambda e: e.tensor_scalar(col(7), col(3), 1e-5, None, ALU.add), waits=[vc], inc=vc)
        SC(lambda e: e.activation(col(4), col(7), AF.Ln), waits=[vc], inc=sc)
        SC(lambda e: e.activation(col(4), col(4), AF.Exp, scale=-0.5), waits=[sc], inc=sc)
        V(lambda e: e.reciprocal(col(5), col(7)), waits=[vc, sc], inc=vc)
        V(lambda e: e.tensor_tensor(col(5), col(3), col(5), ALU.mult), waits=[vc], inc=vc)
        V(lambda e: e.tensor_scalar(col(6), col(5), 1e-6, None, ALU.add), waits=[vc], inc=vc)
        SC(lambda e: e.activation(col(6), col(6), AF.Ln), waits=[vc], inc=sc)
        SC(lambda e: e.activation(col(6), col(6), AF.Exp, scale=-0.5), waits=[sc], inc=sc)
        V(lambda e: e.tensor_scalar(col(5), col(5), 1e-5, None, ALU.add), waits=[sc], inc=vc)
        SC(lambda e: e.activation(col(5), col(5), AF.Ln), waits=[vc], inc=sc)
        SC(lambda e: e.activation(col(5), col(5), AF.Exp, scale=-0.5), waits=[sc], inc=sc)
        V(lambda e: e.tensor_tensor(col(5), col(4), col(5), ALU.mult), waits=[sc], inc=vc)
        V(lambda e: e.tensor_tensor(col(6), col(4), col(6), ALU.mult), waits=[vc], inc=vc)
        stats_done = (vc, vc.v)

        stg_rel = [(stsp[c], 0) for c in range(ND)]
        xrelA = list(xrel)
        trA_rel = [(sc, 0), (sc, 0), (vc, 0), (vc, 0)]
        trB_rel = [(vc, 0), (vc, 0)]
        for i in range(n_tt):
            sl = i % 2
            Y(lambda e, sl=sl, i=i, b=b: e.dma_start(out=xt[sl], in_=x_h[b, 128 * i:128 * (i + 1), :]),
              waits=[xrelA[sl]], inc=xld[sl], incn=16)
            ldw = (xld[sl], xld[sl].v)
            V(lambda e, sl=sl, i=i: e.tensor_scalar(
                xnb, xt[sl], cs[:, 32 + i:33 + i], cs[:, 64 + i:65 + i], ALU.subtract, ALU.mult),
              waits=[ldw, stats_done, (tc_, tc_.v)], inc=vc)
            V(lambda e, sl=sl, i=i: e.tensor_scalar(
                x2p, xt[sl], cs[:, 32 + i:33 + i], cs[:, 80 + i:81 + i], ALU.subtract, ALU.mult),
              waits=[(tc_, tc_.v)], inc=vc)
            V(lambda e, sl=sl, i=i: e.tensor_scalar(
                nxp, xt[sl], cs[:, 32 + i:33 + i], cs[:, 96 + i:97 + i], ALU.subtract, ALU.mult),
              waits=[(tc_, tc_.v)], inc=vc)
            xrelA[sl] = (vc, vc.v)
            apv = vc.v
            for c in range(ND):
                tb = c % 2
                T(lambda e, c=c, tb=tb: e.transpose(
                    bankA(tb, 128), xnb[:, 128 * c:128 * (c + 1)], ident),
                  waits=[(vc, apv), trA_rel[tb], consts_ld], inc=tc_)
                t1 = tc_.v
                T(lambda e, c=c, tb=tb: e.transpose(
                    bankA(2 + tb, 128), x2p[:, 128 * c:128 * (c + 1)], ident),
                  waits=[trA_rel[2 + tb]], inc=tc_)
                t2 = tc_.v
                T(lambda e, c=c, tb=tb: e.transpose(
                    bankB(tb, 128), nxp[:, 128 * c:128 * (c + 1)], ident),
                  waits=[trB_rel[tb]], inc=tc_)
                t3 = tc_.v
                SC(lambda e, c=c, i=i, tb=tb: e.activation(
                    xnstg[c][:, 128 * (i % 4):128 * (i % 4 + 1)], bankA(tb, 128), AF.Copy),
                   waits=[(tc_, t1), stg_rel[c]], inc=sc)
                trA_rel[tb] = (sc, sc.v)
                V(lambda e, c=c, i=i, tb=tb: e.tensor_scalar(
                    x2T[:, S * c + 128 * i: S * c + 128 * (i + 1)], bankA(2 + tb, 128),
                    lncol[:, c:c + 1], lncol[:, 8 + c:9 + c], ALU.mult, ALU.add),
                  waits=[(tc_, t2)], inc=vc)
                trA_rel[2 + tb] = (vc, vc.v)
                V(lambda e, c=c, i=i, tb=tb: e.tensor_scalar(
                    nxstg[c][:, 128 * (i % 4):128 * (i % 4 + 1)], bankB(tb, 128),
                    lncol[:, 16 + c:17 + c], None, ALU.mult),
                  waits=[(tc_, t3), stg_rel[c]], inc=vc)
                trB_rel[tb] = (vc, vc.v)
            if i % 4 == 3:
                for c in range(ND):
                    Y(lambda e, c=c, i=i: e.dma_start(
                        out=xnsp[c, :, 512 * (i // 4):512 * (i // 4) + 512], in_=xnstg[c]),
                      waits=[(sc, sc.v), (vc, vc.v)], inc=stsp[c], incn=16)
                    Y(lambda e, c=c, i=i: e.dma_start(
                        out=nxsp[c, :, 512 * (i // 4):512 * (i // 4) + 512], in_=nxstg[c]),
                      waits=[(vc, vc.v)], inc=stsp[c], incn=16)
                    stg_rel[c] = (stsp[c], stsp[c].v)
        a_spill_done = [(stsp[c], stsp[c].v) for c in range(ND)]
        x2t_done = (vc, vc.v)
        ar.pop()
        p.fence()

        ar.push()
        wsl = [[ar.bf16(128, D) for _ in range(2)] for _ in range(3)]
        qraw = [ar.f32(128, 512) for _ in range(2)]
        kraw = [ar.f32(128, 512) for _ in range(2)]
        rowa = ar.f32(2, 512)
        rowb_ = ar.f32(2, 512)
        sqs = ar.bf16(128, 512)
        wrel = [[(tc_, 0)] * 2 for _ in range(3)]
        bstate = {"pq": [[], []], "pk": [[], []], "pv": [[], []], "gqc": [0]}
        btail_pending = [None]
        for i in range(n_tt):
            V(lambda e, i=i: e.memset(vbuf[:, 1040 * i:1040 * (i + 1)], 1.0),
              waits=[(tc_, tc_.v)], inc=vc)
        V(lambda e: e.memset(vbuf[:, 1040 * n_tt:1040 * n_tt + 63], 1.0),
          waits=[(tc_, tc_.v)], inc=vc)
        vbuf_init = (vc, vc.v)

        for h in range(NPAIR):
            sl = h % 2
            wsems = (wql, wkl, wvl)
            for wi, wh in enumerate((wq_h, wk_h, wv_h)):
                G(lambda e, sl=sl, wi=wi, wh=wh, h=h: e.dma_start(
                    out=wsl[wi][sl], in_=wh[:, D * h:D * (h + 1)]),
                  waits=[wrel[wi][sl]], inc=wsems[wi][sl], incn=16)
            wq_ld = (wql[sl], wql[sl].v)
            wk_ld = (wkl[sl], wkl[sl].v)
            wv_ld = (wvl[sl], wvl[sl].v)
            bq_lh = biasr[0:1, 128 * h:128 * (h + 1)]
            bk_lh = biasr[0:1, D + 128 * h:D + 128 * (h + 1)]
            bv_rh = biasr[0:1, 2 * D + 128 * h:2 * D + 128 * (h + 1)]
            pq_rel = bstate["pq"]
            pk_rel = bstate["pk"]
            pv_rel = bstate["pv"]

            def flush_btail(info):
                h3, bqv_, qc2, qsl2, vsl2, q_ready2, k_ready2, tv2, cols2 = info
                for tt4 in range(4):
                    i2 = 4 * qc2 + tt4
                    for hh2 in range(2):
                        V(lambda e, i2=i2, tt4=tt4, vsl2=vsl2, hh2=hh2, h=h3: e.tensor_copy(
                            vbuf[:, 1040 * i2 + 65 * (2 * h + hh2):1040 * i2 + 65 * (2 * h + hh2) + 64],
                            psb[:, 512 * vsl2 + 128 * tt4 + 64 * hh2:512 * vsl2 + 128 * tt4 + 64 * hh2 + 64]),
                          waits=[(tc_, tv2), vbuf_init])
                pv_rel[vsl2] = [(vc, vc.v)]
                V(lambda e, qsl2=qsl2: e.tensor_copy(qraw[qsl2], bankA(qsl2)),
                  waits=[(tc_, q_ready2)])
                qcopy_v = vc.v
                SC(lambda e, qsl2=qsl2: e.activation(sqs, bankA(qsl2), AF.Square),
                   waits=[(tc_, q_ready2), (tc_, tc_.v), (vc, vc.v)], inc=sc)
                T(lambda e: e.matmul(psb[0:2, 1024:1536], ind2, sqs, start=True, stop=True),
                  waits=[(sc, sc.v), (vc, vc.v)], inc=tc_)
                V(lambda e: e.tensor_scalar(rowa, psb[0:2, 1024:1536], 1e-24, None, ALU.max),
                  waits=[(tc_, tc_.v), (sc, sc.v)])
                SC(lambda e: e.activation(rowb_, rowa, AF.Ln), waits=[(vc, vc.v)], inc=sc)
                SC(lambda e: e.activation(rowa, rowb_, AF.Exp, scale=-0.5), inc=sc)
                T(lambda e: e.matmul(bankB(2), e2, rowa[0:2, :], start=True, stop=True),
                  waits=[(sc, sc.v), (vc, vc.v), consts_ld], inc=tc_)
                V(lambda e, h=h3, cols2=cols2, qsl2=qsl2: e.tensor_tensor(
                    qbuf[:, S * h + cols2.start:S * h + cols2.stop], qraw[qsl2], bankB(2), ALU.mult),
                  waits=[(tc_, tc_.v)])
                pq_rel[qsl2] = [(sc, sc.v), (vc, qcopy_v)]
                V(lambda e, qsl2=qsl2: e.tensor_copy(kraw[qsl2], bankA(2 + qsl2)),
                  waits=[(tc_, k_ready2)])
                kcopy_v = vc.v
                SC(lambda e, qsl2=qsl2: e.activation(sqs, bankA(2 + qsl2), AF.Square),
                   waits=[(tc_, k_ready2), (vc, vc.v), (tc_, tc_.v)], inc=sc)
                T(lambda e: e.matmul(psb[0:2, 1536:2048], ind2, sqs, start=True, stop=True),
                  waits=[(sc, sc.v)], inc=tc_)
                V(lambda e: e.tensor_scalar(rowa, psb[0:2, 1536:2048], 1e-24, None, ALU.max),
                  waits=[(tc_, tc_.v), (sc, sc.v)])
                SC(lambda e: e.activation(rowb_, rowa, AF.Ln), waits=[(vc, vc.v)], inc=sc)
                SC(lambda e: e.activation(rowa, rowb_, AF.Exp, scale=-0.5), inc=sc)
                T(lambda e: e.matmul(bankB(3), e2, rowa[0:2, :], start=True, stop=True),
                  waits=[(sc, sc.v), (vc, vc.v)], inc=tc_)
                V(lambda e, h=h3, cols2=cols2, qsl2=qsl2: e.tensor_tensor(
                    kbuf[:, S * h + cols2.start:S * h + cols2.stop], kraw[qsl2], bankB(3), ALU.mult),
                  waits=[(tc_, tc_.v)])
                pk_rel[qsl2] = [(sc, sc.v), (vc, kcopy_v)]

            for qc in range(n_qc):
                qsl = bstate["gqc"][0] % 2
                bstate["gqc"][0] += 1
                cols = slice(512 * qc, 512 * (qc + 1))
                for c in range(ND):
                    T(lambda e, c=c, qsl=qsl, cols=cols, sl=sl: e.matmul(
                        bankA(qsl), wsl[0][sl][:, 128 * c:128 * (c + 1)],
                        x2T[:, S * c + cols.start:S * c + cols.stop],
                        start=(c == 0), stop=False),
                      waits=([wq_ld, x2t_done, (vc, consts_v), consts_lg] + pq_rel[qsl]) if c == 0 else [])
                T(lambda e, qsl=qsl, bq_lh=bq_lh: e.matmul(
                    bankA(qsl), bq_lh, ones_row, start=False, stop=True), inc=tc_)
                q_ready = tc_.v
                for c in range(ND):
                    T(lambda e, c=c, qsl=qsl, cols=cols, sl=sl: e.matmul(
                        bankA(2 + qsl), wsl[1][sl][:, 128 * c:128 * (c + 1)],
                        x2T[:, S * c + cols.start:S * c + cols.stop],
                        start=(c == 0), stop=False),
                      waits=([wk_ld] + pk_rel[qsl]) if c == 0 else [])
                T(lambda e, qsl=qsl, bk_lh=bk_lh: e.matmul(
                    bankA(2 + qsl), bk_lh, ones_row, start=False, stop=True), inc=tc_)
                k_ready = tc_.v
                vsl = qsl
                for tt4 in range(4):
                    i = 4 * qc + tt4
                    for c in range(ND):
                        T(lambda e, c=c, i=i, tt4=tt4, vsl=vsl, sl=sl: e.matmul(
                            psb[:, 512 * vsl + 128 * tt4:512 * vsl + 128 * (tt4 + 1)],
                            x2T[:, S * c + 128 * i:S * c + 128 * (i + 1)],
                            wsl[2][sl][:, 128 * c:128 * (c + 1)],
                            start=(c == 0), stop=False),
                          waits=([wv_ld] + pv_rel[vsl]) if (c == 0 and tt4 == 0) else [])
                    T(lambda e, tt4=tt4, vsl=vsl, bv_rh=bv_rh: e.matmul(
                        psb[:, 512 * vsl + 128 * tt4:512 * vsl + 128 * (tt4 + 1)],
                        ones_row[0:1, 0:128], bv_rh, start=False, stop=True),
                      inc=(tc_ if tt4 == 3 else None))
                tv = tc_.v
                if btail_pending[0] is not None:
                    fn0, info0 = btail_pending[0]
                    fn0(info0)
                btail_pending[0] = (flush_btail, (h, 0, qc, qsl, vsl, q_ready, k_ready, tv, cols))
            for wi in range(3):
                wrel[wi][sl] = (tc_, tc_.v)
                wrel[wi][sl] = (tc_, tc_.v)
        if btail_pending[0] is not None:
            fn0, info0 = btail_pending[0]
            fn0(info0)
            btail_pending[0] = None
        ar.pop()
        p.fence()

        ar.push()
        kaugs = [ar.bf16(128, S) for _ in range(2)]
        staugs = [ar.bf16(128, S) for _ in range(2)]
        att = [ar.bf16(128, HL) for _ in range(4)]
        sqc = ar.bf16(64, S)
        srow2 = ar.bf16(1, S)
        bcf = ar.f32(64, HL)
        stg = [ar.f32(128, S) for _ in range(2)]
        for z in range(2):
            V(lambda e, z=z: e.memset(stg[z][64:128, :], 0.0),
              waits=[(tc_, tc_.v)])
            V(lambda e, z=z: e.memset(kaugs[z][64:128, :], 0.0), waits=[(tc_, tc_.v)])
            V(lambda e, z=z: e.memset(kaugs[z][64:65, :], 1.0))
            V(lambda e, z=z: e.memset(staugs[z][64:128, :], 0.0))
            V(lambda e, z=z: e.memset(staugs[z][64:65, :], -1.0))

        step_vc = [(vc, 0)]
        srow_war = [(vc, 0)]

        def head_copies(hh, step):
            z = hh % 2
            ht, sub = hh // 2, hh % 2
            war = (tc_, tc_.v)
            Y(lambda e, ht=ht, sub=sub, z=z: e.dma_start(
                out=kaugs[z][0:64, :], in_=kbuf[64 * sub:64 * sub + 64, S * ht:S * (ht + 1)]),
              waits=[war], inc=kcl[z], incn=16)
            G(lambda e, ht=ht, sub=sub, z=z: e.dma_start(
                out=staugs[z][0:64, :], in_=qbuf[64 * sub:64 * sub + 64, S * ht:S * (ht + 1)]),
              waits=[war, step_vc[0]], inc=qcl[z], incn=16)
            if step == 1:
                SC(lambda e, ht=ht, sub=sub: e.activation(
                    sqc, qbuf[64 * sub:64 * sub + 64, S * ht:S * (ht + 1)], AF.Square),
                   waits=[step_vc[0], srow_war[0], (tc_, tc_.v)], inc=sc)
                for j4 in range(S // 512):
                    T(lambda e, j4=j4: e.matmul(
                        psa[0:1, 1536:2048], ones64b,
                        sqc[:, 512 * j4:512 * (j4 + 1)], start=True, stop=True),
                      waits=[(sc, sc.v)], inc=tc_)
                    SC(lambda e, j4=j4: e.activation(
                        srow2[0:1, 512 * j4:512 * (j4 + 1)],
                        psa[0:1, 1536:2048], AF.Ln),
                       waits=[(tc_, tc_.v)], inc=sc)
                SC(lambda e: e.activation(srow2, srow2, AF.Exp, scale=0.5),
                   waits=[(sc, sc.v)], inc=sc)
                V(lambda e, z=z: e.tensor_scalar(
                    staugs[z][64:65, :], srow2, -1.0, None, ALU.mult),
                  waits=[(sc, sc.v), war])
                srow_war[0] = (vc, vc.v)
            return [(kcl[z], kcl[z].v), (qcl[z], qcl[z].v), (vc, vc.v)]

        tail_rel = [(vc, 0)]
        sim_rel = [[(sc, sc.v)], [(sc, sc.v)]]
        att_rel = [[(tc_, tc_.v)] for _ in range(4)]
        acc_gate = [[(vc, vc.v)]]
        for step in range(2):
            step_vc[0] = (vc, vc.v)
            heads_ready = [None] * (NHEAD + 1)
            heads_ready[0] = head_copies(0, step)
            pending_tail = None

            def flush_tail(info):
                hh2, z2, stg_v, hin2 = info
                ht2, sub2 = hh2 // 2, hh2 % 2
                for qh2 in range(NQH):
                    qco2 = HL * qh2
                    for j2 in range(HL // 512):
                        T(lambda e, z2=z2, qco2=qco2, j2=j2: e.matmul(
                            psa[0:64, 1024:1536], onescf[64:128, :],
                            stg[z2][64:128, qco2 + 512 * j2:qco2 + 512 * (j2 + 1)],
                            start=True, stop=True),
                          waits=[(sc, stg_v[1]), (vc, vc.v)], inc=tc_)
                        V(lambda e, j2=j2: e.tensor_copy(bcf[:, 512 * j2:512 * (j2 + 1)],
                                                         psa[0:64, 1024:1536]),
                          waits=[(tc_, tc_.v)])
                    V(lambda e, z2=z2, qco2=qco2, ht2=ht2, sub2=sub2: e.tensor_tensor(
                        qbuf[64 * sub2:64 * sub2 + 64, S * ht2 + qco2:S * ht2 + qco2 + HL],
                        bcf, stg[z2][0:64, qco2:qco2 + HL], ALU.mult),
                      waits=[(vc, vc.v)] + hin2)

            for hh in range(NHEAD):
                z = hh % 2
                kaug = kaugs[z]
                stin = staugs[z]
                head_in = heads_ready[hh]
                if hh + 1 < NHEAD:
                    heads_ready[hh + 1] = head_copies(hh + 1, step)
                for qh in range(NQH):
                    qco = HL * qh

                    def emit_sim(c):
                        sb = c % 2
                        for j2 in range(HL // 512):
                            T(lambda e, c=c, sb=sb, j2=j2, qco=qco, stin=stin, kaug=kaug: e.matmul(
                                psb[:, 1024 * sb + 512 * j2:1024 * sb + 512 * (j2 + 1)],
                                kaug[0:128, 128 * c:128 * (c + 1)],
                                stin[0:128, qco + 512 * j2:qco + 512 * (j2 + 1)],
                                start=True, stop=True),
                              waits=(head_in + sim_rel[sb]) if j2 == 0 else [],
                              inc=(tc_ if j2 == HL // 512 - 1 else None))
                        return tc_.v

                    simv = [0] * n_tt
                    simv[0] = emit_sim(0)
                    if n_tt > 1:
                        simv[1] = emit_sim(1)
                    for c in range(n_tt):
                        sb = c % 2
                        ab = c % 4
                        SC(lambda e, ab=ab, sb=sb: e.activation(att[ab], psb[:, 1024 * sb:1024 * sb + HL],
                                                                AF.Exp),
                           waits=[(tc_, simv[c])] + att_rel[ab], inc=sc)
                        scv = sc.v
                        sim_rel[sb] = [(sc, scv)]
                        for j2 in range(HL // 512):
                            T(lambda e, c=c, ab=ab, j2=j2, hh=hh: e.matmul(
                                psa[0:128, 512 * j2:512 * (j2 + 1)],
                                vbuf[:, 1040 * c + 65 * hh:1040 * c + 65 * hh + 128],
                                att[ab][:, 512 * j2:512 * (j2 + 1)],
                                start=(c == 0), stop=(c == n_tt - 1)),
                              waits=([(sc, scv)] + (acc_gate[0] if c == 0 else [])) if j2 == 0 else [],
                              inc=(tc_ if (c == n_tt - 1 and j2 == HL // 512 - 1) else None))
                        att_rel[ab] = [(tc_, tc_.v)]
                        if c + 2 < n_tt:
                            simv[c + 2] = emit_sim(c + 2)
                    st_ready = tc_.v
                    V(lambda e, z=z, qco=qco: e.tensor_copy(
                        stg[z][0:64, qco:qco + HL], psa[0:64, 0:HL]),
                      waits=[(tc_, st_ready)])
                    SC(lambda e, z=z, qco=qco: e.activation(
                        stg[z][64:65, qco:qco + HL], psa[64:65, 0:HL], AF.Ln),
                       waits=[(tc_, st_ready)], inc=sc)
                    acc_gate[0] = [(vc, vc.v), (sc, sc.v)]
                    SC(lambda e, z=z, qco=qco: e.activation(
                        stg[z][64:65, qco:qco + HL], stg[z][64:65, qco:qco + HL],
                        AF.Exp, scale=-1.0), inc=sc)
                stg_v = (vc.v, sc.v)
                if pending_tail is not None:
                    flush_tail(pending_tail)
                pending_tail = (hh, z, stg_v, head_in)
            flush_tail(pending_tail)
        ar.pop()
        p.fence()

        ar.push()
        wosl = [ar.bf16(128, D) for _ in range(2)]
        xnrt = [ar.f32(128, S) for _ in range(2)]
        sstg = [ar.f32(128, 512) for _ in range(2)]
        worel = [(tc_, 0), (tc_, 0)]
        xnrel = [(vc, 0), (vc, 0)]
        srel = [(ssl_[0], 0), (ssl_[1], 0)]
        for dd in range(ND):
            sl = dd % 2
            G(lambda e, sl=sl, dd=dd: e.dma_start(out=wosl[sl], in_=wo_h[:, D * dd:D * (dd + 1)]),
              waits=[worel[sl]], inc=wol[sl], incn=16)
            wo_ld = (wol[sl], wol[sl].v)
            Y(lambda e, sl=sl, dd=dd: e.dma_start(out=xnrt[sl], in_=xnsp[dd]),
              waits=[xnrel[sl]] + a_spill_done, inc=xnl[sl], incn=16)
            xn_ld = (xnl[sl], xnl[sl].v)
            bo_lh = biasr[0:1, 3 * D + 128 * dd:3 * D + 128 * (dd + 1)]
            for qc in range(n_qc):
                ob = qc % 2
                cols = slice(512 * qc, 512 * (qc + 1))
                for c in range(ND):
                    T(lambda e, c=c, ob=ob, cols=cols, sl=sl: e.matmul(
                        bankA(ob), wosl[sl][:, 128 * c:128 * (c + 1)],
                        qbuf[:, S * c + cols.start:S * c + cols.stop],
                        start=(c == 0), stop=False),
                      waits=[wo_ld, (vc, vc.v)] if c == 0 else [])
                T(lambda e, ob=ob, bo_lh=bo_lh: e.matmul(
                    bankA(ob), bo_lh, ones_row, start=False, stop=True), inc=tc_)
                tv = tc_.v
                ssl = qc % 2
                V(lambda e, ob=ob, ssl=ssl, cols=cols, sl=sl: e.tensor_tensor(
                    sstg[ssl], bankA(ob), xnrt[sl][:, cols], ALU.add),
                  waits=[(tc_, tv), xn_ld, srel[ssl]], inc=vc)
                Y(lambda e, ssl=ssl, dd=dd, cols=cols: e.dma_start(
                    out=ssp[dd, :, cols.start:cols.stop], in_=sstg[ssl]),
                  waits=[(vc, vc.v)], inc=ssl_[ssl], incn=16)
                srel[ssl] = (ssl_[ssl], ssl_[ssl].v)
            worel[sl] = (tc_, tc_.v)
            xnrel[sl] = (vc, vc.v)
        s_spill_done = [(ssl_[0], ssl_[0].v), (ssl_[1], ssl_[1].v)]
        ar.pop()
        ar.pop()

        p.fence()
        ar.push()
        outb = ar.f32(128, ND * TH)
        for th in range(NTH):
            tho = TH * th
            ar.push()
            nxh = ar.bf16(128, ND * TH)
            wfsl = [ar.bf16(128, FSC) for _ in range(2)]
            wsl1 = [t[:, 0:8 * FC] for t in wfsl]
            wsl3 = [t[:, 8 * FC:16 * FC] for t in wfsl]
            wsl2 = [t[:, 16 * FC:16 * FC + KC * D] for t in wfsl]
            b1sl = [t[:, 16 * FC + KC * D:17 * FC + KC * D] for t in wfsl]
            b3sl = [t[:, 17 * FC + KC * D:18 * FC + KC * D] for t in wfsl]
            a2sl = [t[:, 18 * FC + KC * D:18 * FC + KC * D + KC * 128] for t in wfsl]
            trsel = [[ar.bf16(128, TH) for _ in range(2)] for _ in range(2)]
            wexp = [ar.f32(128, TH) for _ in range(2)]
            mex = [ar.f32(128, TH) for _ in range(2)]
            rrs = [ar.f32(128, TH) for _ in range(2)]
            rrb = [ar.bf16(128, TH) for _ in range(2)]
            act2w = [ar.bf16(128, KC * TC) for _ in range(2)]
            silu_s = ar.f32(128, KC * TC)
            h3w_s = ar.f32(128, KC * TC)
            glog = ar.f32(128, 8)
            m1c = ar.f32(128, 1)
            m2c = ar.f32(128, 1)
            e2c = ar.f32(128, 1)
            w1c = ar.f32(128, 1)
            w2c_ = ar.f32(128, 1)
            msk = [ar.f32(128, 8) for _ in range(2)]
            tmp8 = ar.f32(128, 8)
            mrow = [ar.f32(8, TH) for _ in range(2)]
            wrow = [ar.f32(1, TH) for _ in range(2)]

            for c in range(ND):
                Y(lambda e, c=c, tho=tho: e.dma_start(
                    out=nxh[:, TH * c:TH * (c + 1)], in_=nxsp[c, :, tho:tho + TH]),
                  waits=a_spill_done, inc=nxl, incn=16)
            nx_ld = (nxl, nxl.v)

            for jt in range(TH // 128):
                gb = jt % 2
                for c in range(ND):
                    T(lambda e, c=c, jt=jt, gb=gb: e.matmul(
                        psa[:, 512 * gb:512 * gb + 8],
                        nxh[:, TH * c + 128 * jt:TH * c + 128 * (jt + 1)],
                        gw[:, 8 * c:8 * (c + 1)],
                        start=(c == 0), stop=(c == ND - 1)),
                      waits=[nx_ld, (vc, vc.v), consts_lg] if c == 0 else [],
                      inc=(tc_ if c == ND - 1 else None))
                gv = tc_.v
                V(lambda e, gb=gb: e.tensor_copy(glog, psa[:, 512 * gb:512 * gb + 8]),
                  waits=[(tc_, gv), (vc, vc.v)], inc=vc)
                V(lambda e: e.reduce_max(m1c, glog, axis=AX.X), waits=[(vc, vc.v)], inc=vc)
                V(lambda e: e.tensor_scalar(msk[0], glog, m1c, None, ALU.is_equal),
                  waits=[(vc, vc.v)], inc=vc)
                V(lambda e: e.tensor_scalar(tmp8, msk[0], -1e30, None, ALU.mult),
                  waits=[(vc, vc.v)], inc=vc)
                V(lambda e: e.tensor_tensor(tmp8, glog, tmp8, ALU.add), waits=[(vc, vc.v)], inc=vc)
                V(lambda e: e.reduce_max(m2c, tmp8, axis=AX.X), waits=[(vc, vc.v)], inc=vc)
                V(lambda e: e.tensor_scalar(msk[1], glog, m2c, None, ALU.is_equal),
                  waits=[(vc, vc.v)], inc=vc)
                V(lambda e: e.tensor_tensor(e2c, m2c, m1c, ALU.subtract), waits=[(vc, vc.v)], inc=vc)
                SC(lambda e: e.activation(e2c, e2c, AF.Exp), waits=[(vc, vc.v)], inc=sc)
                V(lambda e: e.tensor_scalar(w1c, e2c, 1.0, None, ALU.add), waits=[(sc, sc.v)], inc=vc)
                V(lambda e: e.reciprocal(w1c, w1c), waits=[(vc, vc.v)], inc=vc)
                V(lambda e: e.tensor_tensor(w2c_, e2c, w1c, ALU.mult), waits=[(vc, vc.v)], inc=vc)
                mv = vc.v
                for j in range(2):
                    T(lambda e, j=j: e.transpose(
                        psa[0:8, 1024 + 128 * j:1024 + 128 * (j + 1)], msk[j], ident),
                      waits=[(vc, mv), (vc, vc.v)], inc=tc_)
                    V(lambda e, j=j, jt=jt: e.tensor_copy(
                        mrow[j][:, 128 * jt:128 * (jt + 1)],
                        psa[0:8, 1024 + 128 * j:1024 + 128 * (j + 1)]),
                      waits=[(tc_, tc_.v)], inc=vc)
                for j, wcol in enumerate((w1c, w2c_)):
                    T(lambda e, j=j, wcol=wcol: e.transpose(
                        psa[0:1, 1536 + 128 * j:1536 + 128 * (j + 1)], wcol, ident),
                      waits=[(vc, mv), (vc, vc.v)], inc=tc_)
                    V(lambda e, j=j, jt=jt: e.tensor_copy(
                        wrow[j][:, 128 * jt:128 * (jt + 1)],
                        psa[0:1, 1536 + 128 * j:1536 + 128 * (j + 1)]),
                      waits=[(tc_, tc_.v)], inc=vc)
            rows_done = (vc, vc.v)
            for j in range(2):
                for ch in range(TH // 512):
                    T(lambda e, j=j, ch=ch: e.matmul(
                        bankB(0), ones1f, wrow[j][0:1, 512 * ch:512 * (ch + 1)],
                        start=True, stop=True),
                      waits=[rows_done, (vc, vc.v)], inc=tc_)
                    V(lambda e, j=j, ch=ch: e.tensor_copy(
                        wexp[j][:, 512 * ch:512 * (ch + 1)], bankB(0)),
                      waits=[(tc_, tc_.v)], inc=vc)
                    T(lambda e, j=j, ch=ch: e.matmul(
                        bankB(1), e8, mrow[j][0:8, 512 * ch:512 * (ch + 1)],
                        start=True, stop=True),
                      waits=[(vc, vc.v)], inc=tc_)
                    V(lambda e, j=j, ch=ch: e.tensor_copy(
                        mex[j][:, 512 * ch:512 * (ch + 1)], bankB(1)),
                      waits=[(tc_, tc_.v)], inc=vc)
            bc_done = (vc, vc.v)

            for br, a_sb in enumerate((a1t, a3t)):
                for ch in range(TH // 512):
                    tb = ch % 2
                    for c in range(ND):
                        T(lambda e, c=c, ch=ch, tb=tb, a_sb=a_sb: e.matmul(
                            bankA(tb), a_sb[:, 128 * c:128 * (c + 1)],
                            nxh[:, TH * c + 512 * ch:TH * c + 512 * (ch + 1)],
                            start=(c == 0), stop=(c == ND - 1)),
                          waits=[(vc, vc.v), consts_lg] if c == 0 else [],
                          inc=(tc_ if c == ND - 1 else None))
                    trv = tc_.v
                    for j in range(2):
                        V(lambda e, br=br, j=j, ch=ch, tb=tb: e.tensor_tensor(
                            trsel[br][j][:, 512 * ch:512 * (ch + 1)], bankA(tb),
                            mex[j][:, 512 * ch:512 * (ch + 1)], ALU.mult),
                          waits=[(tc_, trv), bc_done], inc=vc)
            trsel_done = (vc, vc.v)
            for j in range(2):
                V(lambda e, j=j: e.memset(rrs[j][:, :], 0.0), waits=[(tc_, tc_.v)], inc=vc)
            rr_zero = (vc, vc.v)

            w_rel = [(tc_, 0), (tc_, 0)]
            grel = [[], []]
            for fc in range(NFC):
                sl = fc % 2
                G(lambda e, sl=sl, fc=fc: e.dma_start(
                    out=wfsl[sl], in_=wfs_h[:, FSC * fc:FSC * (fc + 1)]),
                  waits=[w_rel[sl]], inc=w1l[sl], incn=16)
                sl_ld = [(w1l[sl], w1l[sl].v)]
                for tcc in range(n_tc):
                    tco = TC * tcc
                    G0, G1 = bankA, bankB
                    silu_v = [0, 0]
                    a2v = [0, 0]

                    def emit_h(j, grp, gwaits):
                        h1v = 0
                        for br in range(2):
                            wslb = (wsl1 if br == 0 else wsl3)[sl]
                            bslb = (b1sl if br == 0 else b3sl)[sl]
                            for m in range(KC):
                                bank_i = KC * br + m
                                for c in range(ND):
                                    T(lambda e, c=c, m=m, grp=grp, bank_i=bank_i,
                                        wslb=wslb, tco=tco: e.matmul(
                                        grp(bank_i),
                                        wslb[:, FC * c + 128 * m:FC * c + 128 * (m + 1)],
                                        nxh[:, TH * c + tco:TH * c + tco + TC],
                                        start=(c == 0), stop=False),
                                      waits=gwaits if (c == 0 and m == 0 and br == 0) else [])
                                T(lambda e, m=m, br=br, j=j, grp=grp, bank_i=bank_i,
                                    bslb=bslb, tco=tco: e.matmul(
                                    grp(bank_i),
                                    bslb[:, 128 * m:128 * (m + 1)],
                                    trsel[br][j][:, tco:tco + TC],
                                    start=False, stop=True),
                                  inc=(tc_ if m == KC - 1 else None))
                                if br == 0 and m == KC - 1:
                                    h1v = tc_.v
                        return h1v, tc_.v

                    def emit_evac(j, grp, h1v, hv):
                        for bi in range(KC):
                            SC(lambda e, bi=bi, grp=grp: e.activation(
                                silu_s[:, 512 * bi:512 * (bi + 1)], grp(bi), AF.Silu),
                               waits=[(tc_, h1v), (vc, vc.v)], inc=sc)
                        silu_v[j] = sc.v
                        for bi in range(KC):
                            V(lambda e, bi=bi, grp=grp, j=j, tco=tco: e.tensor_tensor(
                                h3w_s[:, 512 * bi:512 * (bi + 1)],
                                grp(KC + bi),
                                wexp[j][:, tco:tco + TC], ALU.mult),
                              waits=[(tc_, hv)])
                        V(lambda e, j=j: e.tensor_tensor(act2w[j], silu_s, h3w_s, ALU.mult),
                          waits=[(sc, silu_v[j])])
                        a2v[j] = vc.v

                    def emit_rr(j, rbank):
                        for c4 in range(KC):
                            T(lambda e, c4=c4, j=j, rbank=rbank, sl=sl: e.matmul(
                                rbank, a2sl[sl][:, 128 * c4:128 * (c4 + 1)],
                                act2w[j][:, TC * c4:TC * (c4 + 1)],
                                start=(c4 == 0), stop=(c4 == KC - 1)),
                              waits=([(vc, a2v[j]), (sc, silu_v[1]), (vc, vc.v)] + sl_ld)
                              if c4 == 0 else [],
                              inc=(tc_ if c4 == KC - 1 else None))
                        rrv = tc_.v
                        V(lambda e, j=j, rrv=rrv, rbank=rbank, tco=tco: e.tensor_tensor(
                            rrs[j][:, tco:tco + TC], rbank, rrs[j][:, tco:tco + TC], ALU.add),
                          waits=[(tc_, rrv), rr_zero])

                    def emit_w2_half(grp, gten, ddh):
                        for dd4 in range(4):
                            dd = 4 * ddh + dd4
                            for j in range(2):
                                for c4 in range(KC):
                                    T(lambda e, dd=dd, dd4=dd4, c4=c4, j=j, grp=grp,
                                        sl=sl: e.matmul(
                                        grp(dd4),
                                        wsl2[sl][:, D * c4 + 128 * dd:D * c4 + 128 * (dd + 1)],
                                        act2w[j][:, TC * c4:TC * (c4 + 1)],
                                        start=(j == 0 and c4 == 0), stop=(j == 1 and c4 == KC - 1)),
                                      waits=([(vc, a2v[1]), (sc, silu_v[1]), (vc, a2v[0]),
                                              (vc, vc.v)] + sl_ld)
                                      if (c4 == 0 and dd4 == 0 and j == 0) else [],
                                      inc=(tc_ if (j == 1 and c4 == KC - 1) else None))
                            wv = tc_.v
                            if fc == 0:
                                V(lambda e, dd=dd, dd4=dd4, grp=grp, tco=tco, wv=wv: e.tensor_copy(
                                    outb[:, TH * dd + tco:TH * dd + tco + TC], grp(dd4)),
                                  waits=[(tc_, wv)])
                            else:
                                V(lambda e, dd=dd, dd4=dd4, grp=grp, tco=tco, wv=wv: e.tensor_tensor(
                                    outb[:, TH * dd + tco:TH * dd + tco + TC], grp(dd4),
                                    outb[:, TH * dd + tco:TH * dd + tco + TC], ALU.add),
                                  waits=[(tc_, wv)])

                    h1v0, hv0 = emit_h(0, G0, sl_ld + [trsel_done] + grel[0])
                    emit_evac(0, G0, h1v0, hv0)
                    h1v1, hv1 = emit_h(1, G1, grel[1])
                    emit_evac(1, G1, h1v1, hv1)
                    emit_rr(0, G1(0, TC))
                    emit_w2_half(G0, psa, 0)
                    emit_rr(1, G1(1, TC))
                    emit_w2_half(G1, psb, 1)
                    grel[0] = [(vc, vc.v)]
                    grel[1] = [(vc, vc.v)]
                w_rel[sl] = (tc_, tc_.v)
            for j in range(2):
                V(lambda e, j=j: e.tensor_tensor(rrs[j], rrs[j], mex[j], ALU.mult),
                  waits=[(vc, vc.v)], inc=vc)
                V(lambda e, j=j: e.tensor_copy(rrb[j], rrs[j]), waits=[(vc, vc.v)], inc=vc)
            rbv = vc.v
            for tcc in range(TH // 512):
                tco = 512 * tcc
                for dd in range(ND):
                    bank_i = dd % 4
                    for j in range(2):
                        T(lambda e, j=j, dd=dd, tco=tco, bank_i=bank_i: e.matmul(
                            bankA(bank_i), b2c[:, 128 * dd:128 * (dd + 1)],
                            rrb[j][:, tco:tco + 512],
                            start=(j == 0), stop=(j == 1)),
                          waits=[(vc, rbv), (vc, vc.v), consts_lg] if j == 0 else [],
                          inc=(tc_ if j == 1 else None))
                    V(lambda e, dd=dd, tco=tco, bank_i=bank_i: e.tensor_tensor(
                        outb[:, TH * dd + tco:TH * dd + tco + 512],
                        bankA(bank_i),
                        outb[:, TH * dd + tco:TH * dd + tco + 512], ALU.add),
                      waits=[(tc_, tc_.v)], inc=vc)
            ar.pop()
            p.fence()
            ffn_done = (vc, vc.v)

            ar.push()
            sgh = ar.f32(128, ND * TH)
            stile = [ar.f32(128, TH) for _ in range(2)]
            gtile = ar.f32(128, TH)
            ofin = [ar.f32(128, D) for _ in range(2)]
            xrt = [ar.f32(128, D) for _ in range(2)]
            strel = [(vc, 0), (vc, 0)]
            for dd in range(ND):
                sl = dd % 2
                Y(lambda e, dd=dd, sl=sl, tho=tho: e.dma_start(
                    out=stile[sl], in_=ssp[dd, :, tho:tho + TH]),
                  waits=[strel[sl]] + s_spill_done, inc=stl[sl], incn=16)
                st_ld = (stl[sl], stl[sl].v)
                SC(lambda e, dd=dd: e.activation(gtile, outb[:, TH * dd:TH * (dd + 1)], AF.Silu),
                   waits=[ffn_done, (vc, vc.v)], inc=sc)
                V(lambda e, dd=dd, sl=sl: e.tensor_tensor(
                    sgh[:, TH * dd:TH * (dd + 1)], stile[sl], gtile, ALU.mult),
                  waits=[(sc, sc.v), st_ld], inc=vc)
                strel[sl] = (vc, vc.v)
            sg_done = (vc, vc.v)
            orel_f = [(ofs[0], 0), (ofs[1], 0)]
            for jt in range(TH // 128):
                osl = jt % 2
                Y(lambda e, jt=jt, osl=osl, tho=tho, b=b: e.dma_start(
                    out=xrt[osl], in_=x_h[b, tho + 128 * jt:tho + 128 * (jt + 1), :]),
                  waits=[orel_f[osl]], inc=xfl[osl], incn=16)
                xf_ld = (xfl[osl], xfl[osl].v)
                for dd in range(ND):
                    tb = dd % 2
                    T(lambda e, jt=jt, dd=dd, tb=tb: e.transpose(
                        bankB(tb, 128), sgh[:, TH * dd + 128 * jt:TH * dd + 128 * (jt + 1)], ident),
                      waits=[sg_done, (vc, vc.v)], inc=tc_)
                    V(lambda e, jt=jt, dd=dd, osl=osl, tb=tb: e.tensor_tensor(
                        ofin[osl][:, 128 * dd:128 * (dd + 1)], bankB(tb, 128),
                        xrt[osl][:, 128 * dd:128 * (dd + 1)], ALU.add),
                      waits=[(tc_, tc_.v), xf_ld], inc=vc)
                Y(lambda e, jt=jt, osl=osl, tho=tho, b=b: e.dma_start(
                    out=out_h[b, tho + 128 * jt:tho + 128 * (jt + 1), :], in_=ofin[osl]),
                  waits=[(vc, vc.v)], inc=ofs[osl], incn=16)
                orel_f[osl] = (ofs[osl], ofs[osl].v)
            ar.pop()
            p.fence()
        ar.pop()

    final_sems = [(ofs[0], ofs[0].v), (ofs[1], ofs[1].v),
                  (ssl_[0], ssl_[0].v), (ssl_[1], ssl_[1].v)] +                  [(stsp[c], stsp[c].v) for c in range(ND)]
    p.op("sync", lambda e: None, waits=final_sems)
    p.finalize()
    return nc


def prep_weights(i):
    def bf(x):
        return np.ascontiguousarray(x, dtype=np.float32)

    w = {}
    for nm, W in (("wq", i["L0_attn_Wq"]), ("wk", i["L0_attn_Wk"]),
                  ("wv", i["L0_attn_Wv"]), ("wo", i["L0_attn_Wo"])):
        w[nm] = bf(np.asarray(W).reshape(8, 128, 8, 128).transpose(1, 2, 0, 3).reshape(128, 8192))
    w["bias"] = bf(np.concatenate(
        [np.asarray(i["L0_attn_b" + k]).reshape(1, D) for k in "qkv"]
        + [np.asarray(i["L0_attn_bo"]).reshape(1, D)], axis=1))
    lnw = np.asarray(i["L0_attn_ln_w"]).reshape(8, 128).T
    lnb = np.asarray(i["L0_attn_ln_b"]).reshape(8, 128).T
    nwc = np.asarray(i["L0_ffn_norm_w"]).reshape(8, 128).T
    w["lncol"] = bf(np.concatenate([lnw, lnb, nwc], axis=1))
    w1t = np.asarray(i["L0_ffn_w1"]).T
    w3t = np.asarray(i["L0_ffn_w3"]).T
    w["w1t"] = bf(w1t.reshape(8, 128, NFC, FC).transpose(1, 2, 0, 3).reshape(128, NFC * 8 * FC))
    w["w3t"] = bf(w3t.reshape(8, 128, NFC, FC).transpose(1, 2, 0, 3).reshape(128, NFC * 8 * FC))
    w2t = np.asarray(i["L0_ffn_w2"]).T
    w["w2t"] = bf(w2t.reshape(NFC, KC, 128, D).transpose(2, 0, 1, 3).reshape(128, NFC * KC * D))
    w["b1c"] = bf(np.asarray(i["L0_ffn_b1"]).transpose(0, 2, 1).reshape(ER, FF))
    w["b3c"] = bf(np.asarray(i["L0_ffn_b3"]).transpose(0, 2, 1).reshape(ER, FF))
    a2t = np.asarray(i["L0_ffn_a2"]).reshape(ER, FF).T
    w["a2t"] = bf(a2t.reshape(NFC, KC, 128, ER).transpose(2, 0, 1, 3).reshape(128, NFC * KC * ER))
    a1t = np.asarray(i["L0_ffn_a1"]).reshape(ER, D).T
    a3t = np.asarray(i["L0_ffn_a3"]).reshape(ER, D).T
    w["a1t"] = bf(a1t.reshape(8, 128, 128).transpose(1, 0, 2).reshape(128, 8 * 128))
    w["a3t"] = bf(a3t.reshape(8, 128, 128).transpose(1, 0, 2).reshape(128, 8 * 128))
    w["b2c"] = bf(np.asarray(i["L0_ffn_b2"]).transpose(0, 2, 1).reshape(ER, D))
    FSC = 8 * FC + 8 * FC + KC * D + FC + FC + KC * 128
    fs = np.empty((128, NFC * FSC), np.float32)
    for fc in range(NFC):
        o = fc * FSC
        fs[:, o:o + 8 * FC] = w["w1t"][:, 8 * FC * fc:8 * FC * (fc + 1)]
        o += 8 * FC
        fs[:, o:o + 8 * FC] = w["w3t"][:, 8 * FC * fc:8 * FC * (fc + 1)]
        o += 8 * FC
        fs[:, o:o + KC * D] = w["w2t"][:, KC * D * fc:KC * D * (fc + 1)]
        o += KC * D
        fs[:, o:o + FC] = w["b1c"][:, FC * fc:FC * (fc + 1)]
        o += FC
        fs[:, o:o + FC] = w["b3c"][:, FC * fc:FC * (fc + 1)]
        o += FC
        fs[:, o:o + KC * 128] = w["a2t"][:, KC * 128 * fc:KC * 128 * (fc + 1)]
    w["wfs"] = np.ascontiguousarray(fs)
    gwt = np.asarray(i["L0_ffn_gate_w"]).T
    w["gw"] = bf(gwt.reshape(8, 128, 8).transpose(1, 0, 2).reshape(128, 64))
    e2m = np.zeros((2, 128), np.float32)
    e2m[0, 0:64] = 1.0
    e2m[1, 64:128] = 1.0
    w["e2"] = e2m
    e8m = np.zeros((8, 128), np.float32)
    for ee in range(8):
        e8m[ee, 16 * ee:16 * (ee + 1)] = 2.0
    w["e8"] = e8m
    w["ident"] = np.eye(128, dtype=np.float32)
    return w


_CACHE = {}


def kernel(**inputs):
    S, NB, NCORE = 2048, 2, 8
    key = (S, NB)
    if key not in _CACHE:
        _CACHE[key] = build_program(S, NB)
    nc = _CACHE[key]
    w = prep_weights(inputs)
    x = np.ascontiguousarray(np.asarray(inputs["L0_attn_hidden_states"], dtype=np.float32))
    in_maps = []
    for cid in range(NCORE):
        m = dict(w)
        m["x"] = np.ascontiguousarray(x[NB * cid:NB * (cid + 1)])
        in_maps.append(m)
    res = run_bass_kernel_spmd(nc, in_maps, core_ids=list(range(NCORE)))
    out = np.concatenate([np.asarray(r["out"], dtype=np.float32) for r in res.results], axis=0)
    return out
